# version 05
# baseline (speedup 1.0000x reference)
import math
from contextlib import ExitStack

import numpy as np
import ml_dtypes

import concourse.bass as bass
import concourse.mybir as mybir
from concourse.bass_utils import run_bass_kernel_spmd

F32 = mybir.dt.float32
BF16 = mybir.dt.bfloat16
I32 = mybir.dt.int32
AF = mybir.ActivationFunctionType
ALU = mybir.AluOpType
AX = mybir.AxisListType

C = 1024
HD = 64
HPC = 4
GROUPS = [[0, 1, 2, 3], [4, 5, 6, 7]]
LN_EPS = 1e-5
SHIFT_MARGIN = 60.0
RT2 = math.sqrt(2.0)
TWO_PI = 2.0 * math.pi
ENGS = ("sync", "gpsimd", "tensor", "scalar", "vector")


def op(name, *args, **kw):
    f = lambda e: getattr(e, name)(*args, **kw)
    f.opname = name
    f.opargs = args
    return f


def build(T=8192):
    assert T % 2048 == 0
    TSL = T // 4
    NGRP = T // 512
    NTILP = TSL // 128
    NSTR = TSL // 512

    nc = bass.Bass("TRN2", target_bir_lowering=False)

    x_seq = nc.dram_tensor("x_seq", [T, C], F32, kind="ExternalInput").ap()
    x_res = nc.dram_tensor("x_res", [TSL + 128, C], F32, kind="ExternalInput").ap()
    w_qkv = nc.dram_tensor("w_qkv", [768, C], F32, kind="ExternalInput").ap()
    b_qkv = nc.dram_tensor("b_qkv", [1, 768], F32, kind="ExternalInput").ap()
    w_proj = nc.dram_tensor("w_proj", [C, C], F32, kind="ExternalInput").ap()
    b_proj = nc.dram_tensor("b_proj", [1, C], F32, kind="ExternalInput").ap()
    w_fr = nc.dram_tensor("w_fr", [C, C], F32, kind="ExternalInput").ap()
    w_fk = nc.dram_tensor("w_fk", [C, C], F32, kind="ExternalInput").ap()
    w_fv = nc.dram_tensor("w_fv", [C, C], F32, kind="ExternalInput").ap()
    mu_r = nc.dram_tensor("mu_r", [C], F32, kind="ExternalInput").ap()
    mu_k = nc.dram_tensor("mu_k", [C], F32, kind="ExternalInput").ap()
    identb = nc.dram_tensor("identb", [128, 128], BF16, kind="ExternalInput").ap()
    maskd = nc.dram_tensor("maskd", [128, 896], BF16, kind="ExternalInput").ap()
    halo_s = nc.dram_tensor("halo_s", [128, 1], F32, kind="ExternalInput").ap()
    out_t = nc.dram_tensor("out", [TSL, C], F32, kind="ExternalOutput").ap()

    QSPAN = T // 4
    cc_in_q = [nc.dram_tensor(f"cc_in{i}", [HPC * HD, QSPAN], BF16,
                              kind="Internal").ap() for i in range(4)]
    cc_out_all = nc.dram_tensor("cc_out_all", [4 * C, QSPAN], BF16,
                                kind="Internal").ap()
    w_bf = [nc.dram_tensor(f"w_bf{i}", [C, C], BF16, kind="Internal").ap()
            for i in range(4)]
    wq_bf = nc.dram_tensor("wq_bf", [768, C], BF16, kind="Internal").ap()

    q = {e: [] for e in ENGS}
    sems = {}
    cnt = {}

    def E(eng, f):
        q[eng].append(f)

    def W(eng, sem, val):
        if val and val > 0:
            q[eng].append(lambda e, s=sems[sem], v=val: e.wait_ge(s, v))

    def I(eng, sem, f, k=1):
        cnt[sem] += k
        q[eng].append(lambda e, f=f, s=sems[sem], kk=k: f(e).then_inc(s, kk))
        return cnt[sem]

    ctx = ExitStack()
    with ctx:
        def sb(st, name, shape, dt, side="left"):
            return st.enter_context(nc.sbuf_tensor(name, shape, dt, side=side))

        def psum(st, name, shape, dt=F32):
            return st.enter_context(nc.psum_tensor(name, shape, dt))

        for s in ("ldc", "ldx0", "ldx1", "ldx2", "ldx3", "ldx4", "ldx5",
                  "ldx6", "ldx7", "ldw", "ldwq",
                  "ldmu", "ldw2", "gp", "pe", "act", "dve", "sS", "sE", "sM",
                  "sV",
                  "sty0", "sty1", "scc", "ldyh", "ldy", "ldwt", "ldr0",
                  "ldr1",
                  "sto0", "sto1"):
            sems[s] = ctx.enter_context(nc.semaphore(s))
            cnt[s] = 0

        s_ident = sb(ctx, "s_ident", [128, 128], BF16)
        s_maskd = sb(ctx, "s_maskd", [128, 896], BF16)
        s_ones_b = sb(ctx, "s_ones_b", [1, 512], BF16)
        s_ones_c = sb(ctx, "s_ones_c", [128, 1], BF16)
        s_ones64 = sb(ctx, "s_ones64", [65, 64], F32)
        s_ones_f = sb(ctx, "s_ones_f", [1, 128], F32)
        s_bqkv = sb(ctx, "s_bqkv", [1, 768], BF16)
        s_bproj = sb(ctx, "s_bproj", [1, C], BF16)
        s_mur = sb(ctx, "s_mur", [128, 8], F32)
        s_muk = sb(ctx, "s_muk", [128, 8], F32)
        s_halo = sb(ctx, "s_halo", [128, 1], F32)
        s_shiftn = sb(ctx, "s_shiftn", [128, 1], F32)
        s_eps = sb(ctx, "s_eps", [128, 1], F32)
        s_b8 = sb(ctx, "s_b8", [64, 1], F32)
        s_nq = sb(ctx, "s_nq", [1, 2 * NGRP], F32)
        s_nk = sb(ctx, "s_nk", [1, 2 * NGRP], F32)
        s_sm = sb(ctx, "s_sm", [1, 8], F32)

        block = ctx.enter_context(nc.Block())

        st1 = ExitStack()
        q_sb = [sb(st1, f"q_sb{p}", [128, T], BF16) for p in range(2)]
        k_sb = [sb(st1, f"k_sb{p}", [128, T], BF16) for p in range(2)]
        v_sb = [sb(st1, f"v_sb{h}", [128, (T // 128) * 65], BF16)
                for h in range(HPC)]

        st0 = ExitStack()
        s_wqkvT = sb(st0, "s_wqkvT", [128, 8 * 768], BF16)
        f_tab = sb(st0, "f_tab", [128, T], BF16, side="right")
        s_theta = sb(st0, "s_theta", [1, 64], F32)
        s_seq = sb(st0, "s_seq", [1, 256], F32)
        s_u = sb(st0, "s_u", [64, 256], F32)
        s_kf = sb(st0, "s_kf", [64, 256], F32)
        s_d0 = sb(st0, "s_d0", [64, 256], F32)
        x_ring = [sb(st0, f"x_ring{i}", [128, C], F32) for i in range(8)]
        sq_scr = [sb(st0, f"sq_scr{i}", [128, C], BF16) for i in range(2)]
        xln_tm = [sb(st0, f"xln_tm{i}", [128, C], BF16) for i in range(2)]
        xln_fm = [sb(st0, f"xln_fm{i}", [128, 8 * 512], BF16)
                  for i in range(2)]
        stats_sq = [sb(st0, f"stats_sq{i}", [128, 4], F32) for i in range(2)]
        stats_su = [sb(st0, f"stats_su{i}", [128, 4], F32) for i in range(2)]
        stats_m = [sb(st0, f"stats_m{i}", [128, 4], F32) for i in range(2)]
        stats_v = sb(st0, "stats_v", [128, 4], F32)
        stats_t = [sb(st0, f"stats_t{i}", [128, 4], F32) for i in range(2)]
        stats_r = [sb(st0, f"stats_r{i}", [128, 4], F32) for i in range(2)]
        s_nsq = [sb(st0, f"s_nsq{i}", [128, 512], BF16) for i in range(2)]

        ps01 = ExitStack()
        tr_ps = [psum(ps01, f"tr_ps{i}", [128, 128], BF16) for i in range(2)]
        qk_ps = [psum(ps01, f"qk_ps{i}", [128, 512]) for i in range(4)]
        v_ps = psum(ps01, "v_ps", [128, 512])
        nrm_ps = psum(ps01, "nrm_ps", [1, 512])

        for tns, val in ((s_ones_b, 1.0), (s_ones_c, 1.0), (s_ones_f, 1.0),
                         (s_ones64, 1.0),
                         (s_eps, LN_EPS), (s_b8, 0.125), (s_sm, 0.0)):
            E("gpsimd", op("memset", tns[:], val))
        for h in range(HPC):
            E("gpsimd", op("memset", v_sb[h][:], 1.0))
        gp0 = I("gpsimd", "gp", op("iota", s_theta[:, 0:32], [[1, 32]],
                                   base=0, channel_multiplier=0,
                                   allow_small_or_imprecise_dtypes=True))
        I("gpsimd", "ldw", op("dma_start", out=s_bqkv[:], in_=b_qkv), 16)
        I("gpsimd", "ldw", op("dma_start", out=s_bproj[:], in_=b_proj), 16)
        ldw_wq = I("gpsimd", "ldw", op(
            "dma_start", out=wq_bf, in_=w_qkv), 16)
        for c8 in range(8):
            I("gpsimd", "ldmu", op("dma_start", out=s_mur[:, c8:c8 + 1],
                                   in_=mu_r[c8 * 128:(c8 + 1) * 128],
                                   allow_slow_non_contiguous=True), 16)
            I("gpsimd", "ldmu", op("dma_start", out=s_muk[:, c8:c8 + 1],
                                   in_=mu_k[c8 * 128:(c8 + 1) * 128],
                                   allow_slow_non_contiguous=True), 16)
        ldmu_all = cnt["ldmu"]

        I("sync", "ldc", op("dma_start", out=s_ident[:], in_=identb), 16)
        I("sync", "ldc", op("dma_start", out=s_maskd[:], in_=maskd), 16)
        ldc_all = I("sync", "ldc", op("dma_start", out=s_halo[:], in_=halo_s), 16)

        W("scalar", "gp", gp0)
        a_th = I("scalar", "act", op("activation", s_theta[:, 0:32],
                                     s_theta[:, 0:32], AF.Exp,
                                     scale=-math.log(10000.0) / 32.0))
        W("vector", "act", a_th)
        d_th = I("vector", "dve", op("tensor_copy", s_theta[:, 32:64],
                                     s_theta[:, 0:32]))

        tr_evac = {-2: 0, -1: 0}
        W("scalar", "ldw", cnt["ldw"])
        for icc in range(8):
            I("scalar", "ldwq", op(
                "dma_start", out=s_wqkvT[:, icc * 768:(icc + 1) * 768],
                in_=wq_bf[:, icc * 128:(icc + 1) * 128], transpose=True), 16)
        ldwq_all = cnt["ldwq"]

        a_u_prev = 0
        p_ang_prev = 0
        for g5 in range(T // 256):
            sl = slice(g5 * 256, (g5 + 1) * 256)
            if g5 == 0:
                W("tensor", "dve", d_th)
                W("tensor", "gp", gp0)
            else:
                W("tensor", "dve", a_u_prev)
            W("gpsimd", "pe", p_ang_prev)
            sq_i = I("gpsimd", "gp", op("iota", s_seq[:], [[1, 256]],
                                        base=g5 * 256 + 1,
                                        channel_multiplier=0,
                                        allow_small_or_imprecise_dtypes=True))
            W("tensor", "gp", sq_i)
            p_ang = I("tensor", "pe", op("matmul", qk_ps[0][0:64, 0:256],
                                         s_theta[:], s_seq[:],
                                         start=True, stop=True))
            p_ang_prev = p_ang
            W("vector", "pe", p_ang)
            a_u = I("vector", "dve", op("tensor_scalar", s_u[:],
                                        qk_ps[0][0:64, 0:256],
                                        1.0 / TWO_PI, 0.125,
                                        ALU.mult, ALU.add))
            a_u_prev = a_u
            W("vector", "dve", a_u)
            d1 = I("vector", "dve", op("tensor_scalar", s_kf[:], s_u[:],
                                       8388608.0, 8388608.0,
                                       ALU.add, ALU.subtract))
            W("vector", "dve", d1)
            d2 = I("vector", "dve", op("tensor_sub", s_d0[:], s_u[:], s_kf[:]))
            W("scalar", "dve", d2)
            a_s = I("scalar", "act", op("activation", s_u[:], s_d0[:], AF.Sin,
                                        scale=TWO_PI))
            W("vector", "act", a_s)
            d3 = I("vector", "dve", op("tensor_scalar_mul", f_tab[0:64, sl],
                                       s_u[:], RT2))
            W("vector", "dve", d3)
            I("vector", "dve", op("tensor_copy", f_tab[64:128, sl],
                                  f_tab[0:64, sl]))

        x_load, app_c, tr_c = {}, {}, {}
        sq_act = {}
        grp_chain5 = {}
        grp_c4 = {}
        qk_evac, v_evac = {}, {}
        grp_chain1 = {}
        last_nsq_pe = {0: 0, 1: 0}
        last_nrm_red = 0

        def ln_prep(g):
            gm = g % 2
            sqa = {}
            for tix in range(4):
                tile = 4 * g + tix
                if tile >= 8:
                    W("sync", "dve", app_c[tile - 8])
                x_load[tile] = I("sync", f"ldx{tile % 8}", op(
                    "dma_start", out=x_ring[tile % 8][:],
                    in_=x_seq[tile * 128:(tile + 1) * 128, :]), 16)
                W("scalar", f"ldx{tile % 8}", x_load[tile])
                if g >= 2 and tix == 0:
                    W("scalar", "dve", grp_chain1[g - 2])
                if tile >= 1:
                    W("scalar", "act", sq_act[tile - 1])
                I("scalar", "act", op(
                    "activation", sq_scr[tile % 2][:], x_ring[tile % 8][:],
                    AF.Square, accum_out=stats_sq[gm][:, tix:tix + 1]))
                sqa[tix] = I("scalar", "act", op(
                    "activation", sq_scr[(tile + 1) % 2][:],
                    x_ring[tile % 8][:],
                    AF.Copy, accum_out=stats_su[gm][:, tix:tix + 1]))
                sq_act[tile] = sqa[tix]

            W("vector", "act", sqa[3])
            if g >= 2:
                W("vector", "dve", grp_chain5[g - 2])
            E("vector", op("tensor_scalar_mul", stats_m[gm][:],
                           stats_su[gm][:], 1.0 / C))
            c1 = I("vector", "dve", op("tensor_scalar_mul", stats_v[:],
                                       stats_sq[gm][:], 1.0 / C))
            grp_chain1[g] = c1
            W("vector", "dve", c1)
            c2 = I("vector", "dve", op("scalar_tensor_tensor", stats_t[gm][:],
                                       stats_m[gm][:], -1.0, stats_m[gm][:],
                                       ALU.mult, ALU.mult))
            W("vector", "dve", c2)
            c3 = I("vector", "dve", op("tensor_add", stats_v[:], stats_v[:],
                                       stats_t[gm][:]))
            W("scalar", "dve", c3)
            c4 = I("scalar", "act", op("activation", stats_t[gm][:],
                                       stats_v[:], AF.Sqrt, bias=s_eps[:]))
            grp_c4[g] = c4
            return c4

        ln_prep(0)
        for g in range(NGRP):
            gm = g % 2
            W("vector", "act", grp_c4[g])
            c5 = I("vector", "dve", op("reciprocal", stats_r[gm][:],
                                       stats_t[gm][:]))
            grp_chain5[g] = c5
            if g + 1 < NGRP:
                ln_prep(g + 1)
            for tix in range(4):
                tile = 4 * g + tix
                W("vector", "dve", c5)
                if tile >= 2:
                    W("vector", "pe", tr_c[tile - 2])
                app_c[tile] = I("vector", "dve", op(
                    "tensor_scalar", xln_tm[tile % 2][:], x_ring[tile % 8][:],
                    stats_m[gm][:, tix:tix + 1], stats_r[gm][:, tix:tix + 1],
                    ALU.subtract, ALU.mult))
                for c8 in range(8):
                    i = tile * 8 + c8
                    if i == 0:
                        W("tensor", "ldc", ldc_all)
                    if c8 == 0:
                        W("tensor", "dve", app_c[tile])
                        if tix == 0 and g > 1:
                            W("tensor", "dve", qk_evac[g - 2])
                            W("tensor", "dve", v_evac[(g - 2, 1)])
                    W("tensor", "dve", tr_evac[i - 2])
                    p_tr = I("tensor", "pe", op(
                        "transpose", tr_ps[i % 2][:],
                        xln_tm[tile % 2][:, c8 * 128:(c8 + 1) * 128],
                        s_ident[:]))
                    W("vector", "pe", p_tr)
                    tr_evac[i] = I("vector", "dve", op(
                        "tensor_copy",
                        xln_fm[g % 2][:, c8 * 512 + tix * 128:
                                       c8 * 512 + (tix + 1) * 128],
                        tr_ps[i % 2][:]))
                tr_c[tile] = cnt["pe"]
            evac32 = cnt["dve"]

            W("tensor", "dve", evac32)
            if g == 0:
                W("tensor", "ldwq", ldwq_all)
            if g > 0:
                W("tensor", "dve", qk_evac[g - 1])
            p_qk = None
            for p in range(2):
                for path in range(2):
                    bank = qk_ps[p * 2 + path]
                    ocs = path * 256 + p * 128
                    for icc in range(8):
                        E("tensor", op(
                            "matmul", bank[:],
                            s_wqkvT[:, icc * 768 + ocs: icc * 768 + ocs + 128],
                            xln_fm[g % 2][:, icc * 512:(icc + 1) * 512],
                            start=(icc == 0), stop=False))
                    p_qk = I("tensor", "pe", op(
                        "matmul", bank[:], s_bqkv[:, ocs:ocs + 128],
                        s_ones_b[:], start=False, stop=True))
            W("vector", "pe", p_qk)
            sl = slice(g * 512, (g + 1) * 512)
            for p in range(2):
                E("vector", op("tensor_mul", q_sb[p][:, sl], qk_ps[p * 2][:],
                               f_tab[:, sl]))
                qk_evac[g] = I("vector", "dve", op(
                    "tensor_mul", k_sb[p][:, sl], qk_ps[p * 2 + 1][:],
                    f_tab[:, sl]))

            for half in range(2):
                if half == 0:
                    W("tensor", "dve", evac32)
                    if g > 0:
                        W("tensor", "dve", v_evac[(g - 1, 1)])
                else:
                    W("tensor", "dve", v_evac[(g, 0)])
                p_v = None
                for sub2 in range(2):
                    sub = half * 2 + sub2
                    vslc = v_ps[:, sub2 * 256:(sub2 + 1) * 256]
                    for icc in range(8):
                        E("tensor", op(
                            "matmul", vslc,
                            xln_fm[g % 2][:, icc * 512 + sub * 128:
                                          icc * 512 + (sub + 1) * 128],
                            s_wqkvT[:, icc * 768 + 512: icc * 768 + 768],
                            start=(icc == 0), stop=False))
                    p_v = I("tensor", "pe", op(
                        "matmul", vslc, s_ones_b[0:1, 0:128],
                        s_bqkv[:, 512:768], start=False, stop=True))
                W("vector", "pe", p_v)
                ev = None
                for sub2 in range(2):
                    kblk = g * 4 + half * 2 + sub2
                    for h in range(HPC):
                        ev = I("vector", "dve", op(
                            "tensor_copy",
                            v_sb[h][:, kblk * 65: kblk * 65 + 64],
                            v_ps[:, sub2 * 256 + h * 64:
                                 sub2 * 256 + (h + 1) * 64]))
                v_evac[(g, half)] = ev

            for p in range(2):
                for path in range(2):
                    src = (q_sb if path == 0 else k_sb)[p]
                    nm = path
                    W("scalar", "dve", qk_evac[g])
                    W("scalar", "pe", last_nsq_pe[nm])
                    a_n = I("scalar", "act", op(
                        "activation", s_nsq[nm][:], src[:, sl], AF.Square))
                    W("tensor", "act", a_n)
                    W("tensor", "dve", last_nrm_red)
                    p_n = I("tensor", "pe", op(
                        "matmul", nrm_ps[:], s_ones_c[:], s_nsq[nm][:],
                        start=True, stop=True))
                    last_nsq_pe[nm] = p_n
                    W("vector", "pe", p_n)
                    dst = (s_nq if path == 0 else s_nk)
                    last_nrm_red = I("vector", "dve", op(
                        "reduce_max", dst[0:1, 2 * g + p: 2 * g + p + 1],
                        nrm_ps[:], axis=AX.X))

        W("vector", "dve", last_nrm_red)
        E("vector", op("reduce_max", s_sm[0:1, 0:1], s_nq[:], axis=AX.X))
        sc1 = I("vector", "dve", op("reduce_max", s_sm[0:1, 1:2], s_nk[:],
                                    axis=AX.X))
        W("vector", "dve", sc1)
        sc2 = I("vector", "dve", op("tensor_mul", s_sm[0:1, 2:3],
                                    s_sm[0:1, 0:1], s_sm[0:1, 1:2]))
        W("scalar", "dve", sc2)
        sc3 = I("scalar", "act", op("activation", s_sm[0:1, 3:4],
                                    s_sm[0:1, 2:3], AF.Sqrt))
        W("vector", "act", sc3)
        sc4 = I("vector", "dve", op("tensor_scalar", s_sm[0:1, 4:5],
                                    s_sm[0:1, 3:4], -0.125, SHIFT_MARGIN,
                                    ALU.mult, ALU.add))
        W("vector", "dve", sc4)
        sc5 = I("vector", "dve", op("tensor_scalar_min", s_sm[0:1, 5:6],
                                    s_sm[0:1, 4:5], 0.0))
        W("tensor", "dve", sc5)
        W("tensor", "dve", qk_evac[NGRP - 1])
        p_sh = I("tensor", "pe", op("matmul", qk_ps[0][:, 0:1], s_ones_f[:],
                                    s_sm[0:1, 5:6], start=True, stop=True))
        W("vector", "pe", p_sh)
        I("vector", "dve", op("tensor_copy", s_shiftn[:], qk_ps[0][:, 0:1]))

        ps01.close()
        st0.close()

        ps2 = ExitStack()
        s_cell = [psum(ps2, f"s_cell{i}", [128, 1024]) for i in range(2)]
        y_ps = [psum(ps2, f"y_ps{i}", [65, 512]) for i in range(4)]

        st2 = ExitStack()
        e_cell = [sb(st2, f"e_cell{i}", [128, 1024], BF16) for i in range(3)]
        yev = sb(st2, "yev", [128, 512], F32)
        ynrm = [sb(st2, f"ynrm{i}", [128, 512], BF16) for i in range(2)]
        rcp = sb(st2, "rcp", [65, 512], F32)

        ph1_dve, ph1_pe, ph1_act = cnt["dve"], cnt["pe"], cnt["act"]
        for eng in ("tensor", "scalar", "vector", "gpsimd"):
            W(eng, "dve", ph1_dve)
            W(eng, "pe", ph1_pe)
            W(eng, "act", ph1_act)

        exp_c, attv_c, gate_c = {}, {}, {}
        ser = 0
        ytile = 0
        y_dma, y_mult = {}, {}
        pending_norm = [None]

        QG = NGRP // 4
        w_cast = {}
        for wi, wsrc in enumerate((w_proj, w_fr, w_fk, w_fv)):
            w_cast[wi] = I("gpsimd", "ldw2", op(
                "dma_start", out=w_bf[wi], in_=wsrc), 16)
        for qt in range(NGRP):
            for pair in range(2):
                hA, hB = 2 * pair, 2 * pair + 1
                nkb = 4 * (qt + 1)
                qsl = slice(qt * 512, (qt + 1) * 512)
                base = ser

                def scores_emit(j):
                    sj = base + j
                    cell = s_cell[sj % 2]
                    ksl = slice(j * 128, (j + 1) * 128)
                    if sj >= 2:
                        W("tensor", "sE", exp_c[sj - 2])
                    E("tensor", op("matmul", cell[:, 0:512],
                                   k_sb[pair][0:64, ksl],
                                   q_sb[pair][0:64, qsl],
                                   start=True, stop=True))
                    sS_c = I("tensor", "sS", op(
                        "matmul", cell[:, 512:1024], k_sb[pair][64:128, ksl],
                        q_sb[pair][64:128, qsl], start=True, stop=True))
                    W("scalar", "sS", sS_c)
                    if sj >= 3:
                        W("scalar", "sV", attv_c[sj - 3])
                    exp_c[sj] = I("scalar", "sE", op(
                        "activation", e_cell[sj % 3][:], s_cell[sj % 2][:],
                        AF.Exp, scale=0.125, bias=s_shiftn[:]))
                    if j >= nkb - 4:
                        o384 = 384 - 128 * (j - (nkb - 4))
                        ecl = e_cell[sj % 3]
                        W("vector", "sE", exp_c[sj])
                        E("vector", op("tensor_mul", ecl[:, 0:512],
                                       ecl[:, 0:512],
                                       s_maskd[:, o384:o384 + 512]))
                        gate_c[sj] = ("sM", I("vector", "sM", op(
                            "tensor_mul", ecl[:, 512:1024], ecl[:, 512:1024],
                            s_maskd[:, o384:o384 + 512])))
                    else:
                        gate_c[sj] = ("sE", exp_c[sj])

                def attv_emit(j, yset):
                    sj = base + j
                    ecl = e_cell[sj % 3]
                    gs, gv = gate_c[sj]
                    if gs == "sM":
                        W("tensor", "sE", exp_c[sj])
                    W("tensor", gs, gv)
                    if j == 0 and ytile >= 2:
                        W("tensor", "dve", y_mult[ytile - 2])
                    E("tensor", op("matmul", y_ps[2 * yset][:],
                                   v_sb[hA][:, j * 65:(j + 1) * 65],
                                   ecl[:, 0:512], start=(j == 0),
                                   stop=(j == nkb - 1)))
                    attv_c[sj] = I("tensor", "sV", op(
                        "matmul", y_ps[2 * yset + 1][:],
                        v_sb[hB][:, j * 65:(j + 1) * 65],
                        ecl[:, 512:1024], start=(j == 0),
                        stop=(j == nkb - 1)))

                yset = ytile % 2
                scores_emit(0)
                if nkb > 1:
                    scores_emit(1)
                for j in range(nkb):
                    if j + 2 < nkb:
                        scores_emit(j + 2)
                    if j == min(4, nkb - 1) and pending_norm[0] is not None:
                        pending_norm[0]()
                        pending_norm[0] = None
                    attv_emit(j, yset)
                ser += nkb

                W("vector", "sV", attv_c[ser - 1])
                if ytile >= 1:
                    W("vector", "dve", y_mult[ytile - 1])
                yA, yB = y_ps[2 * yset], y_ps[2 * yset + 1]
                E("vector", op("reciprocal", rcp[0:1, :], yA[64:65, :]))
                E("vector", op("reciprocal", rcp[64:65, :], yB[64:65, :]))
                E("vector", op("tensor_copy", yev[0:64, :], yA[0:64, :]))
                d_ev = I("vector", "dve", op(
                    "tensor_copy", yev[64:128, :], yB[0:64, :]))

                def mk_norm(qt=qt, pair=pair, n=ytile, yset=yset, d_ev=d_ev):
                    def norm():
                        yA, yB = y_ps[2 * yset], y_ps[2 * yset + 1]
                        W("tensor", "dve", d_ev)
                        E("tensor", op("matmul", yA[0:64, :],
                                       s_ones_f[0:1, 0:64], rcp[0:1, :],
                                       start=True, stop=True))
                        p_bc = I("tensor", "pe", op(
                            "matmul", yB[0:64, :], s_ones64[64:65, :],
                            rcp[64:65, :], start=True, stop=True))
                        W("vector", "pe", p_bc)
                        if n >= 2:
                            W("vector", f"sty{n % 2}", y_dma[n - 2])
                        yt = ynrm[n % 2]
                        E("vector", op("tensor_mul", yt[0:64, :],
                                       yev[0:64, :], yA[0:64, :]))
                        y_mult[n] = I("vector", "dve", op(
                            "tensor_mul", yt[64:128, :], yev[64:128, :],
                            yB[0:64, :]))
                        W("sync", "dve", y_mult[n])
                        iq = qt // QG
                        y_dma[n] = I("sync", f"sty{n % 2}", op(
                            "dma_start",
                            out=cc_in_q[iq][pair * 128:(pair + 1) * 128,
                                            (qt % QG) * 512:
                                            (qt % QG + 1) * 512],
                            in_=yt[:]), 16)
                    return norm
                pending_norm[0] = mk_norm()
                ytile += 1
            if qt % QG == QG - 1:
                if pending_norm[0] is not None:
                    pending_norm[0]()
                    pending_norm[0] = None
                iq = qt // QG
                W("gpsimd", "sty0", cnt["sty0"])
                W("gpsimd", "sty1", cnt["sty1"])
                I("gpsimd", "scc", op(
                    "collective_compute", "AllGather", ALU.bypass,
                    replica_groups=GROUPS, ins=[cc_in_q[iq]],
                    outs=[cc_out_all[iq * C:(iq + 1) * C, :]]))

        ph2_dve = cnt["dve"]

        stw = ExitStack()
        w_projT = sb(stw, "w_projT", [128, 8 * 1024], BF16, side="right")
        w_frT = sb(stw, "w_frT", [128, 8 * 1024], BF16, side="right")
        w_fkT = sb(stw, "w_fkT", [128, 8 * 1024], BF16, side="right")
        w_fvT = sb(stw, "w_fvT", [128, 8 * 1024], BF16, side="right")

        ps2.close()
        st2.close()
        st1.close()

        st4 = ExitStack()
        y_tile = [sb(st4, f"y_tile{i}", [128, 8 * 512], BF16)
                  for i in range(2)]
        y_halo = sb(st4, "y_halo", [128, 8 * 128], BF16)
        xr_t = [sb(st4, f"xr_t{i}", [128, C], F32) for i in range(2)]
        out1 = [sb(st4, f"out1{i}", [128, 4 * C], F32) for i in range(2)]
        h2_tm = [sb(st4, f"h2_tm{i}", [128, C], BF16) for i in range(2)]
        h2_fm = sb(st4, "h2_fm", [128, 8 * 513], BF16)
        hstage = sb(st4, "hstage", [128, 8], BF16)
        p4sq = sb(st4, "p4sq", [128, C], BF16)
        p4st = [sb(st4, f"p4st{i}", [128, 5], F32) for i in range(7)]
        d_mix = sb(st4, "d_mix", [128, 8 * 512], BF16)
        xr_fm = sb(st4, "xr_fm", [128, 8 * 512], BF16)
        xk_fm = sb(st4, "xk_fm", [128, 8 * 512], BF16)
        kact = sb(st4, "kact", [128, 8 * 512], BF16)
        ksc = sb(st4, "ksc", [128, 512], F32)
        r_f32 = sb(st4, "r_f32", [128, C], F32)
        fin = [sb(st4, f"fin{i}", [128, C], F32) for i in range(3)]

        ps4 = ExitStack()
        pr_ps = psum(ps4, "pr_ps", [128, 1024])
        t4_ps = [psum(ps4, f"t4_ps{i}", [128, 128], BF16) for i in range(2)]
        rr_ps = psum(ps4, "rr_ps", [128, 1024])
        kv_ps = psum(ps4, "kv_ps", [128, 1024])

        tr4_evac = {-2: 0, -1: 0}
        t4i_box = [0]
        W("scalar", "ldw2", cnt["ldw2"])
        for wi, wdst in enumerate((w_projT, w_frT, w_fkT, w_fvT)):
            for icc in range(8):
                I("scalar", "ldwt", op(
                    "dma_start", out=wdst[:, icc * 1024:(icc + 1) * 1024],
                    in_=w_bf[wi][:, icc * 128:(icc + 1) * 128],
                    transpose=True), 16)
        ldwt_all = cnt["ldwt"]

        def sy_pid(e):
            qv = e.partition_id() % 4
            sy_pid.qrow = e.snap(qv * 8, donate=True, min_val=0, max_val=32)
            sy_pid.hrow = e.snap(((qv + 3) % 4) * 8, donate=True,
                                 min_val=0, max_val=32)
        E("sync", sy_pid)
        cc_view = cc_out_all.rearrange("(r b) t -> b r t", b=128)

        resid_d, xres_ld = {}, {}
        p4sq_act = []
        tr4_done = {}
        last_apply_box = [0]

        def ln2_chain(cols, prev_apply):
            W("vector", "dve", prev_apply)
            E("vector", op("tensor_scalar_mul", p4st[2][:, cols],
                           p4st[0][:, cols], 1.0 / C))
            k1 = I("vector", "dve", op("tensor_scalar_mul", p4st[3][:, cols],
                                       p4st[1][:, cols], 1.0 / C))
            W("vector", "dve", k1)
            k2 = I("vector", "dve", op("scalar_tensor_tensor", p4st[4][:, cols],
                                       p4st[2][:, cols], -1.0, p4st[2][:, cols],
                                       ALU.mult, ALU.mult))
            W("vector", "dve", k2)
            k3 = I("vector", "dve", op("tensor_add", p4st[3][:, cols],
                                       p4st[3][:, cols], p4st[4][:, cols]))
            W("scalar", "dve", k3)
            k4 = I("scalar", "act", op("activation", p4st[4][:, cols],
                                       p4st[3][:, cols], AF.Sqrt,
                                       bias=s_eps[:]))
            W("vector", "act", k4)
            k5 = I("vector", "dve", op("reciprocal", p4st[5][:, cols],
                                       p4st[4][:, cols]))
            return k5

        def emit_proj(tile, lhs_fn, stcol, o1, pe_waits, act_waits=()):
            if tile >= 2:
                W("sync", "dve", resid_d[tile - 2])
            xres_ld[tile] = I("sync", f"ldr{tile % 2}", op(
                "dma_start", out=xr_t[tile % 2][:],
                in_=x_res[tile * 128:(tile + 1) * 128, :]), 16)
            for sm, vv in pe_waits:
                W("tensor", sm, vv)
            if tile >= 1:
                W("tensor", "dve", resid_d.get(tile - 1, 0))
            p_pr = None
            for half in range(2):
                o = pr_ps[:, half * 512:(half + 1) * 512]
                for c8 in range(8):
                    E("tensor", op("matmul", o, lhs_fn(c8),
                                   w_projT[:, c8 * 1024 + half * 512:
                                           c8 * 1024 + half * 512 + 512],
                                   start=(c8 == 0), stop=False))
                p_pr = I("tensor", "pe", op(
                    "matmul", o, s_ones_b[0:1, 0:128],
                    s_bproj[:, half * 512:half * 512 + 512],
                    start=False, stop=True))
            W("vector", "pe", p_pr)
            W("vector", f"ldr{tile % 2}", xres_ld[tile])
            for sm, vv in act_waits:
                W("vector", sm, vv)
            E("vector", op("tensor_add", o1[:, 0:512], pr_ps[:, 0:512],
                           xr_t[tile % 2][:, 0:512]))
            resid_d[tile] = I("vector", "dve", op(
                "tensor_add", o1[:, 512:1024], pr_ps[:, 512:1024],
                xr_t[tile % 2][:, 512:1024]))
            W("scalar", "dve", resid_d[tile])
            if p4sq_act:
                W("scalar", "act", p4sq_act[-1])
            a_sq = I("scalar", "act", op(
                "activation", p4sq[:], o1[:], AF.Square,
                accum_out=p4st[1][:, stcol:stcol + 1]))
            p4sq_act.append(a_sq)
            W("vector", "dve", resid_d[tile])
            d_rs = I("vector", "dve", op(
                "reduce_sum", p4st[0][:, stcol:stcol + 1], o1[:], axis=AX.X))
            return a_sq, d_rs

        def emit_apply_tr(tile, stcol, o1, halo):
            W("vector", "pe", tr4_done.get(tile - 2, 0))
            a_ap = I("vector", "dve", op(
                "tensor_scalar", h2_tm[tile % 2][:], o1[:],
                p4st[2][:, stcol:stcol + 1], p4st[5][:, stcol:stcol + 1],
                ALU.subtract, ALU.mult))
            last_apply_box[0] = a_ap
            for c8 in range(8):
                t4i = t4i_box[0]
                if c8 == 0:
                    W("tensor", "dve", a_ap)
                W("tensor", "dve", tr4_evac[t4i - 2])
                p_tr = I("tensor", "pe", op(
                    "transpose", t4_ps[t4i % 2][:],
                    h2_tm[tile % 2][:, c8 * 128:(c8 + 1) * 128], s_ident[:]))
                W("vector", "pe", p_tr)
                if halo:
                    tr4_evac[t4i] = I("vector", "dve", op(
                        "tensor_scalar_mul", h2_fm[:, c8 * 513:c8 * 513 + 1],
                        t4_ps[t4i % 2][:, 127:128], s_halo[:, 0:1]))
                else:
                    sub = (tile - 1) % 4
                    tr4_evac[t4i] = I("vector", "dve", op(
                        "tensor_copy",
                        h2_fm[:, c8 * 513 + 1 + sub * 128:
                              c8 * 513 + 1 + (sub + 1) * 128],
                        t4_ps[t4i % 2][:]))
                t4i_box[0] += 1
            tr4_done[tile] = cnt["pe"]

        W("sync", "scc", 4)
        yhl = I("sync", "ldyh", (
            lambda e: e.dma_start(
                out=y_halo[:].rearrange("p (a t) -> p a t", t=128),
                in_=cc_view[:, bass.ds(sy_pid.hrow, 8),
                            QSPAN - 128:QSPAN])), 8 * 16)
        out1h = out1[1][:, 0:C]
        a_sq, d_rs = emit_proj(
            0, lambda c8: y_halo[:, c8 * 128:(c8 + 1) * 128], 4, out1h,
            pe_waits=(("ldwt", ldwt_all), ("ldyh", yhl)))
        W("vector", "act", a_sq)
        W("vector", "dve", d_rs)
        ch = ln2_chain(slice(4, 5), 0)
        W("vector", "dve", ch)
        emit_apply_tr(0, 4, out1h, halo=True)
        halo_sq_act = a_sq
        halo_apply = last_apply_box[0]

        proj_pe_done = {}
        f2_prev = [0]
        mix_done_of = {}
        stripe_f2 = {}
        stripe_pe_end = 0
        kact_act = {}
        last_sig = {}
        fin_store = {}
        fin_m = {}

        def emit_ffn(sf):
            nonlocal stripe_pe_end, mix_done_of
            o1 = out1[sf % 2]
            for occ in range(8):
                par = occ % 2
                kslc = kv_ps[:, par * 512:(par + 1) * 512]
                if occ == 0:
                    W("tensor", "dve", mix_done_of[sf])
                if occ >= 2:
                    W("tensor", "act", kact_act.get((sf, occ - 2), 0))
                p_k = None
                for icc in range(8):
                    f = op("matmul", kslc,
                           w_fkT[:, icc * 1024 + occ * 128:
                                 icc * 1024 + (occ + 1) * 128],
                           xk_fm[:, icc * 512:(icc + 1) * 512],
                           start=(icc == 0), stop=(icc == 7))
                    if icc == 7:
                        p_k = I("tensor", "pe", f)
                    else:
                        E("tensor", f)
                W("vector", "pe", p_k)
                if occ >= 1:
                    W("vector", "act", kact_act[(sf, occ - 1)])
                elif sf > 0:
                    W("vector", "act", kact_act[(sf - 1, 7)])
                d_rl = I("vector", "dve", op("tensor_scalar_max", ksc[:],
                                             kslc, 0.0))
                W("scalar", "dve", d_rl)
                if sf > 0 and occ == 0:
                    W("scalar", "pe", stripe_pe_end)
                kact_act[(sf, occ)] = I("scalar", "act", op(
                    "activation", kact[:, occ * 512:(occ + 1) * 512], ksc[:],
                    AF.Square))

            for sub in range(4):
                idx = sf * 4 + sub
                if sub == 0:
                    W("tensor", "dve", mix_done_of[sf])
                else:
                    W("tensor", "act", last_sig[idx - 1])
                p_r = None
                for half in range(2):
                    o = rr_ps[:, half * 512:(half + 1) * 512]
                    for c8 in range(8):
                        f = op("matmul", o,
                               xr_fm[:, c8 * 512 + sub * 128:
                                     c8 * 512 + (sub + 1) * 128],
                               w_frT[:, c8 * 1024 + half * 512:
                                     c8 * 1024 + half * 512 + 512],
                               start=(c8 == 0), stop=(c8 == 7))
                        if half == 1 and c8 == 7:
                            p_r = I("tensor", "pe", f)
                        else:
                            E("tensor", f)
                W("scalar", "pe", p_r)
                if idx >= 1:
                    W("scalar", "dve", fin_m[idx - 1])
                last_sig[idx] = I("scalar", "act", op(
                    "activation", r_f32[:], rr_ps[:], AF.Sigmoid))
                if sub == 0:
                    W("tensor", "act", kact_act[(sf, 7)])
                else:
                    W("tensor", "dve", fin_m[idx - 1])
                p_v = None
                for half in range(2):
                    o = kv_ps[:, half * 512:(half + 1) * 512]
                    for c8 in range(8):
                        f = op("matmul", o,
                               kact[:, c8 * 512 + sub * 128:
                                    c8 * 512 + (sub + 1) * 128],
                               w_fvT[:, c8 * 1024 + half * 512:
                                     c8 * 1024 + half * 512 + 512],
                               start=(c8 == 0), stop=(c8 == 7))
                        if half == 1 and c8 == 7:
                            p_v = I("tensor", "pe", f)
                        else:
                            E("tensor", f)
                W("vector", "pe", p_v)
                W("vector", "act", last_sig[idx])
                if f2_prev[0]:
                    W("vector", "dve", f2_prev[0])
                E("vector", op("tensor_mul", fin[0][:, 0:512],
                               kv_ps[:, 0:512], r_f32[:, 0:512]))
                fin_m[idx] = I("vector", "dve", op(
                    "tensor_mul", fin[0][:, 512:1024], kv_ps[:, 512:1024],
                    r_f32[:, 512:1024]))
                W("vector", "dve", fin_m[idx])
                if idx >= 2:
                    W("vector", f"sto{idx % 2}", fin_store[idx - 2])
                fb = fin[1 + idx % 2]
                E("vector", op("tensor_add", fb[:, 0:512], fin[0][:, 0:512],
                               o1[:, sub * 1024:sub * 1024 + 512]))
                f2 = I("vector", "dve", op(
                    "tensor_add", fb[:, 512:1024], fin[0][:, 512:1024],
                    o1[:, sub * 1024 + 512:(sub + 1) * 1024]))
                f2_prev[0] = f2
                W("sync", "dve", f2)
                fin_store[idx] = I("sync", f"sto{idx % 2}", op(
                    "dma_start",
                    out=out_t[idx * 128:(idx + 1) * 128, :], in_=fb[:]), 16)
            stripe_pe_end = cnt["pe"]
            stripe_f2[sf] = f2_prev[0]

        for s in range(NSTR):
            if s >= 2:
                W("sync", "pe", proj_pe_done[s - 2])
            ylast = I("sync", "ldy", (
                lambda e, s=s: e.dma_start(
                    out=y_tile[s % 2][:].rearrange("p (a t) -> p a t", t=512),
                    in_=cc_view[:, bass.ds(sy_pid.qrow, 8),
                                s * 512:(s + 1) * 512])), 8 * 16)

            st_list = []
            for sub in range(4):
                tile = 1 + 4 * s + sub
                pw = [("ldy", ylast)] if sub == 0 else []
                aw = []
                if s >= 2 and sub == 0:
                    aw = [("dve", stripe_f2[s - 2])]
                a_sq, d_rs = emit_proj(
                    tile,
                    (lambda c8, sub=sub, s=s:
                     y_tile[s % 2][:, c8 * 512 + sub * 128:
                                   c8 * 512 + (sub + 1) * 128]),
                    sub, out1[s % 2][:, sub * 1024:(sub + 1) * 1024],
                    pe_waits=pw, act_waits=aw)
                st_list.append((a_sq, d_rs))
            proj_pe_done[s] = cnt["pe"]
            W("vector", "act", st_list[-1][0])
            W("vector", "dve", st_list[-1][1])
            ch = ln2_chain(slice(0, 4), last_apply_box[0])
            W("vector", "dve", ch)

            if s > 0:
                emit_ffn(s - 1)
                W("vector", "dve", mix_done_of[s - 1])
                hv = h2_fm[:].rearrange("p (c w) -> p c w", w=513)
                b0 = I("vector", "dve", op("tensor_copy", hstage[:],
                                           hv[:, :, 512]))
                W("vector", "dve", b0)
                b1 = I("vector", "dve", op("tensor_copy", hv[:, :, 0],
                                           hstage[:]))
                W("vector", "dve", b1)

            for sub in range(4):
                tile = 1 + 4 * s + sub
                emit_apply_tr(tile, sub,
                              out1[s % 2][:, sub * 1024:(sub + 1) * 1024],
                              halo=False)
            evac_done = cnt["dve"]

            W("vector", "dve", evac_done)
            W("vector", "ldmu", ldmu_all)
            if s > 0:
                W("vector", "pe", stripe_pe_end)
            dm = None
            for c8 in range(8):
                cur = h2_fm[:, c8 * 513 + 1: c8 * 513 + 513]
                prv = h2_fm[:, c8 * 513: c8 * 513 + 512]
                dm = I("vector", "dve", op(
                    "tensor_sub", d_mix[:, c8 * 512:(c8 + 1) * 512], cur, prv))
            W("vector", "dve", dm)
            mx = None
            for c8 in range(8):
                prv = h2_fm[:, c8 * 513: c8 * 513 + 512]
                E("vector", op("scalar_tensor_tensor",
                               xr_fm[:, c8 * 512:(c8 + 1) * 512],
                               d_mix[:, c8 * 512:(c8 + 1) * 512],
                               s_mur[:, c8:c8 + 1], prv, ALU.mult, ALU.add))
                mx = I("vector", "dve", op(
                    "scalar_tensor_tensor", xk_fm[:, c8 * 512:(c8 + 1) * 512],
                    d_mix[:, c8 * 512:(c8 + 1) * 512],
                    s_muk[:, c8:c8 + 1], prv, ALU.mult, ALU.add))
            mix_done_of[s] = mx
        emit_ffn(NSTR - 1)

        W("sync", "sto0", cnt["sto0"])
        W("sync", "sto1", cnt["sto1"])
        W("gpsimd", "sto0", cnt["sto0"])
        W("gpsimd", "sto1", cnt["sto1"])

        for eng in ENGS:
            def body(e, ops=tuple(q[eng]), eng=eng):
                for fi, f in enumerate(ops):
                    try:
                        f(e)
                    except Exception as ex:
                        import traceback
                        traceback.print_exc()
                        raise RuntimeError(
                            f"flush fail {eng}[{fi}]: "
                            f"{getattr(f, 'opname', None) or [getattr(d, 'opname', None) for d in (f.__defaults__ or [])]}"
                        ) from ex
            getattr(block, eng)(body)

        ps4.close()
        st4.close()
        stw.close()

    return nc



def _mask_host():
    p = np.arange(128)[:, None]
    g = np.arange(896)[None, :]
    return (p <= g - 384).astype(ml_dtypes.bfloat16)


def prep_inputs(inp, T):
    x = np.ascontiguousarray(np.asarray(inp["L0_attn_x"], dtype=np.float32))
    W_attn = np.asarray(inp["L0_attn_W_attn"], dtype=np.float32)
    b_attn = np.asarray(inp["L0_attn_b_attn"], dtype=np.float32)
    TSL = T // 4
    shared = dict(
        w_proj=np.ascontiguousarray(np.asarray(inp["L0_attn_W_proj"], np.float32)),
        b_proj=np.asarray(inp["L0_attn_b_proj"], np.float32).reshape(1, C),
        w_fr=np.ascontiguousarray(np.asarray(inp["L0_ffn_W_r"], np.float32)),
        w_fk=np.ascontiguousarray(np.asarray(inp["L0_ffn_W_k"], np.float32)),
        w_fv=np.ascontiguousarray(np.asarray(inp["L0_ffn_W_v"], np.float32)),
        mu_r=np.asarray(inp["L0_ffn_mu_r"], np.float32),
        mu_k=np.asarray(inp["L0_ffn_mu_k"], np.float32),
        identb=np.eye(128, dtype=ml_dtypes.bfloat16),
        maskd=_mask_host(),
    )
    in_maps = []
    for c in range(8):
        seq, qq = c // 4, c % 4
        heads = [4 * qq + h for h in range(HPC)]
        rows, brows = [], []
        for sec in range(3):
            for h in heads:
                rows.append(W_attn[sec * C + h * HD: sec * C + (h + 1) * HD])
                brows.append(b_attn[sec * C + h * HD: sec * C + (h + 1) * HD])
        xs = x[seq, :T]
        t0 = qq * TSL
        halo_rows = (np.zeros((128, C), np.float32) if qq == 0
                     else xs[t0 - 128: t0])
        in_maps.append(dict(
            shared,
            x_seq=np.ascontiguousarray(xs),
            x_res=np.ascontiguousarray(
                np.concatenate([halo_rows, xs[t0: t0 + TSL]], axis=0)),
            w_qkv=np.ascontiguousarray(np.concatenate(rows, axis=0)),
            b_qkv=np.concatenate(brows).reshape(1, 768),
            halo_s=np.full((128, 1), 0.0 if qq == 0 else 1.0, np.float32),
        ))
    return in_maps


_NC_CACHE = {}


def kernel(**inputs):
    T = 8192
    if T not in _NC_CACHE:
        _NC_CACHE[T] = build(T)
    nc = _NC_CACHE[T]
    in_maps = prep_inputs(inputs, T)
    res = run_bass_kernel_spmd(nc, in_maps, core_ids=list(range(8)))
    TSL = T // 4
    outp = np.empty((2, T, C), dtype=np.float32)
    for c in range(8):
        seq, qq = c // 4, c % 4
        outp[seq, qq * TSL:(qq + 1) * TSL] = res.results[c]["out"]
    return outp


# version 06
# speedup vs baseline: 1.1605x; 1.1605x over previous
import math
from contextlib import ExitStack

import numpy as np
import ml_dtypes

import concourse.bass as bass
import concourse.mybir as mybir
from concourse.bass_utils import run_bass_kernel_spmd

F32 = mybir.dt.float32
BF16 = mybir.dt.bfloat16
I32 = mybir.dt.int32
AF = mybir.ActivationFunctionType
ALU = mybir.AluOpType
AX = mybir.AxisListType

C = 1024
HD = 64
HPC = 4
GROUPS = [[0, 1, 2, 3], [4, 5, 6, 7]]
LN_EPS = 1e-5
SHIFT_MARGIN = 60.0
RT2 = math.sqrt(2.0)
TWO_PI = 2.0 * math.pi
ENGS = ("sync", "gpsimd", "tensor", "scalar", "vector")


def op(name, *args, **kw):
    f = lambda e: getattr(e, name)(*args, **kw)
    f.opname = name
    f.opargs = args
    return f


def build(T=8192):
    assert T % 2048 == 0
    TSL = T // 4
    NGRP = T // 512
    NTILP = TSL // 128
    NSTR = TSL // 512

    nc = bass.Bass("TRN2", target_bir_lowering=False)

    x_seq = nc.dram_tensor("x_seq", [T, C], F32, kind="ExternalInput").ap()
    x_res = nc.dram_tensor("x_res", [TSL + 128, C], F32, kind="ExternalInput").ap()
    w_qkv = nc.dram_tensor("w_qkv", [768, C], F32, kind="ExternalInput").ap()
    b_qkv = nc.dram_tensor("b_qkv", [1, 768], F32, kind="ExternalInput").ap()
    w_proj = nc.dram_tensor("w_proj", [C, C], F32, kind="ExternalInput").ap()
    b_proj = nc.dram_tensor("b_proj", [1, C], F32, kind="ExternalInput").ap()
    w_fr = nc.dram_tensor("w_fr", [C, C], F32, kind="ExternalInput").ap()
    w_fk = nc.dram_tensor("w_fk", [C, C], F32, kind="ExternalInput").ap()
    w_fv = nc.dram_tensor("w_fv", [C, C], F32, kind="ExternalInput").ap()
    mu_r = nc.dram_tensor("mu_r", [C], F32, kind="ExternalInput").ap()
    mu_k = nc.dram_tensor("mu_k", [C], F32, kind="ExternalInput").ap()
    identb = nc.dram_tensor("identb", [128, 128], BF16, kind="ExternalInput").ap()
    maskd = nc.dram_tensor("maskd", [128, 896], BF16, kind="ExternalInput").ap()
    halo_s = nc.dram_tensor("halo_s", [128, 1], F32, kind="ExternalInput").ap()
    out_t = nc.dram_tensor("out", [TSL, C], F32, kind="ExternalOutput").ap()

    QSPAN = T // 4
    cc_in_q = [nc.dram_tensor(f"cc_in{i}", [HPC * HD, QSPAN], BF16,
                              kind="Internal").ap() for i in range(4)]
    cc_out_all = nc.dram_tensor("cc_out_all", [4 * C, QSPAN], BF16,
                                kind="Internal").ap()
    w_bf = [nc.dram_tensor(f"w_bf{i}", [C, C], BF16, kind="Internal").ap()
            for i in range(4)]
    wq_bf = nc.dram_tensor("wq_bf", [768, C], BF16, kind="Internal").ap()

    q = {e: [] for e in ENGS}
    sems = {}
    cnt = {}

    def E(eng, f):
        q[eng].append(f)

    def W(eng, sem, val):
        if val and val > 0:
            q[eng].append(lambda e, s=sems[sem], v=val: e.wait_ge(s, v))

    def I(eng, sem, f, k=1):
        cnt[sem] += k
        q[eng].append(lambda e, f=f, s=sems[sem], kk=k: f(e).then_inc(s, kk))
        return cnt[sem]

    ctx = ExitStack()
    with ctx:
        def sb(st, name, shape, dt, side="left"):
            return st.enter_context(nc.sbuf_tensor(name, shape, dt, side=side))

        def psum(st, name, shape, dt=F32):
            return st.enter_context(nc.psum_tensor(name, shape, dt))

        for s in ("ldc", "ldx0", "ldx1", "ldx2", "ldx3", "ldx4", "ldx5",
                  "ldx6", "ldx7", "ldw", "ldwq",
                  "ldmu", "ldw2", "gp", "pe", "act", "dve", "sS", "sE", "sM",
                  "sV",
                  "sty0", "sty1", "scc", "ldyh", "ldy", "ldwt", "ldr0",
                  "ldr1",
                  "sto0", "sto1"):
            sems[s] = ctx.enter_context(nc.semaphore(s))
            cnt[s] = 0

        s_ident = sb(ctx, "s_ident", [128, 128], BF16)
        s_maskd = sb(ctx, "s_maskd", [128, 896], BF16)
        s_ones_b = sb(ctx, "s_ones_b", [1, 512], BF16)
        s_ones_c = sb(ctx, "s_ones_c", [128, 1], BF16)
        s_ones64 = sb(ctx, "s_ones64", [65, 64], F32)
        s_ones_f = sb(ctx, "s_ones_f", [1, 128], F32)
        s_bqkv = sb(ctx, "s_bqkv", [1, 768], BF16)
        s_bproj = sb(ctx, "s_bproj", [1, C], BF16)
        s_mur = sb(ctx, "s_mur", [128, 8], F32)
        s_muk = sb(ctx, "s_muk", [128, 8], F32)
        s_halo = sb(ctx, "s_halo", [128, 1], F32)
        s_shiftn = sb(ctx, "s_shiftn", [128, 1], F32)
        s_eps = sb(ctx, "s_eps", [128, 1], F32)
        s_b8 = sb(ctx, "s_b8", [64, 1], F32)
        s_nq = sb(ctx, "s_nq", [1, 2 * NGRP], F32)
        s_nk = sb(ctx, "s_nk", [1, 2 * NGRP], F32)
        s_sm = sb(ctx, "s_sm", [1, 8], F32)

        block = ctx.enter_context(nc.Block())

        st1 = ExitStack()
        q_sb = [sb(st1, f"q_sb{p}", [128, T], BF16) for p in range(2)]
        k_sb = [sb(st1, f"k_sb{p}", [128, T], BF16) for p in range(2)]
        v_sb = [sb(st1, f"v_sb{h}", [128, (T // 128) * 65], BF16)
                for h in range(HPC)]

        st0 = ExitStack()
        s_wqkvT = sb(st0, "s_wqkvT", [128, 8 * 768], BF16)
        f_tab = sb(st0, "f_tab", [128, T], BF16, side="right")
        s_theta = sb(st0, "s_theta", [1, 64], F32)
        s_seq = sb(st0, "s_seq", [1, 256], F32)
        s_u = sb(st0, "s_u", [64, 256], F32)
        s_kf = sb(st0, "s_kf", [64, 256], F32)
        s_d0 = sb(st0, "s_d0", [64, 256], F32)
        x_ring = [sb(st0, f"x_ring{i}", [128, C], F32) for i in range(8)]
        sq_scr = [sb(st0, f"sq_scr{i}", [128, C], BF16) for i in range(2)]
        xln_tm = [sb(st0, f"xln_tm{i}", [128, C], BF16) for i in range(2)]
        xln_fm = [sb(st0, f"xln_fm{i}", [128, 8 * 512], BF16)
                  for i in range(2)]
        stats_sq = [sb(st0, f"stats_sq{i}", [128, 4], F32) for i in range(2)]
        stats_su = [sb(st0, f"stats_su{i}", [128, 4], F32) for i in range(2)]
        stats_m = [sb(st0, f"stats_m{i}", [128, 4], F32) for i in range(2)]
        stats_v = sb(st0, "stats_v", [128, 4], F32)
        stats_t = [sb(st0, f"stats_t{i}", [128, 4], F32) for i in range(2)]
        stats_r = [sb(st0, f"stats_r{i}", [128, 4], F32) for i in range(2)]
        s_nsq = [sb(st0, f"s_nsq{i}", [128, 512], BF16) for i in range(2)]

        ps01 = ExitStack()
        tr_ps = [psum(ps01, f"tr_ps{i}", [128, 128], BF16) for i in range(2)]
        qk_ps = [psum(ps01, f"qk_ps{i}", [128, 512]) for i in range(4)]
        v_ps = psum(ps01, "v_ps", [128, 512])
        nrm_ps = psum(ps01, "nrm_ps", [1, 512])

        for tns, val in ((s_ones_b, 1.0), (s_ones_c, 1.0), (s_ones_f, 1.0),
                         (s_ones64, 1.0),
                         (s_eps, LN_EPS), (s_b8, 0.125), (s_sm, 0.0)):
            E("gpsimd", op("memset", tns[:], val))
        for h in range(HPC):
            E("gpsimd", op("memset", v_sb[h][:], 1.0))
        gp0 = I("gpsimd", "gp", op("iota", s_theta[:, 0:32], [[1, 32]],
                                   base=0, channel_multiplier=0,
                                   allow_small_or_imprecise_dtypes=True))
        I("gpsimd", "ldw", op("dma_start", out=s_bqkv[:], in_=b_qkv), 16)
        I("gpsimd", "ldw", op("dma_start", out=s_bproj[:], in_=b_proj), 16)
        ldw_wq = I("gpsimd", "ldw", op(
            "dma_start", out=wq_bf, in_=w_qkv), 16)
        for c8 in range(8):
            I("gpsimd", "ldmu", op("dma_start", out=s_mur[:, c8:c8 + 1],
                                   in_=mu_r[c8 * 128:(c8 + 1) * 128],
                                   allow_slow_non_contiguous=True), 16)
            I("gpsimd", "ldmu", op("dma_start", out=s_muk[:, c8:c8 + 1],
                                   in_=mu_k[c8 * 128:(c8 + 1) * 128],
                                   allow_slow_non_contiguous=True), 16)
        ldmu_all = cnt["ldmu"]

        I("sync", "ldc", op("dma_start", out=s_ident[:], in_=identb), 16)
        I("sync", "ldc", op("dma_start", out=s_maskd[:], in_=maskd), 16)
        ldc_all = I("sync", "ldc", op("dma_start", out=s_halo[:], in_=halo_s), 16)

        W("scalar", "gp", gp0)
        a_th = I("scalar", "act", op("activation", s_theta[:, 0:32],
                                     s_theta[:, 0:32], AF.Exp,
                                     scale=-math.log(10000.0) / 32.0))
        W("vector", "act", a_th)
        d_th = I("vector", "dve", op("tensor_copy", s_theta[:, 32:64],
                                     s_theta[:, 0:32]))

        tr_evac = {-2: 0, -1: 0}
        W("scalar", "ldw", cnt["ldw"])
        for icc in range(8):
            I("scalar", "ldwq", op(
                "dma_start", out=s_wqkvT[:, icc * 768:(icc + 1) * 768],
                in_=wq_bf[:, icc * 128:(icc + 1) * 128], transpose=True), 16)
        ldwq_all = cnt["ldwq"]

        a_u_prev = 0
        p_ang_prev = 0
        for g5 in range(T // 256):
            sl = slice(g5 * 256, (g5 + 1) * 256)
            if g5 == 0:
                W("tensor", "dve", d_th)
                W("tensor", "gp", gp0)
            else:
                W("tensor", "dve", a_u_prev)
            W("gpsimd", "pe", p_ang_prev)
            sq_i = I("gpsimd", "gp", op("iota", s_seq[:], [[1, 256]],
                                        base=g5 * 256 + 1,
                                        channel_multiplier=0,
                                        allow_small_or_imprecise_dtypes=True))
            W("tensor", "gp", sq_i)
            p_ang = I("tensor", "pe", op("matmul", qk_ps[0][0:64, 0:256],
                                         s_theta[:], s_seq[:],
                                         start=True, stop=True))
            p_ang_prev = p_ang
            W("vector", "pe", p_ang)
            a_u = I("vector", "dve", op("tensor_scalar", s_u[:],
                                        qk_ps[0][0:64, 0:256],
                                        1.0 / TWO_PI, 0.125,
                                        ALU.mult, ALU.add))
            a_u_prev = a_u
            W("vector", "dve", a_u)
            d1 = I("vector", "dve", op("tensor_scalar", s_kf[:], s_u[:],
                                       8388608.0, 8388608.0,
                                       ALU.add, ALU.subtract))
            W("vector", "dve", d1)
            d2 = I("vector", "dve", op("tensor_sub", s_d0[:], s_u[:], s_kf[:]))
            W("scalar", "dve", d2)
            a_s = I("scalar", "act", op("activation", s_u[:], s_d0[:], AF.Sin,
                                        scale=TWO_PI))
            W("vector", "act", a_s)
            d3 = I("vector", "dve", op("tensor_scalar_mul", f_tab[0:64, sl],
                                       s_u[:], RT2))
            W("vector", "dve", d3)
            I("vector", "dve", op("tensor_copy", f_tab[64:128, sl],
                                  f_tab[0:64, sl]))

        x_load, app_c, tr_c = {}, {}, {}
        sq_act = {}
        grp_chain5 = {}
        grp_c4 = {}
        qk_evac, v_evac = {}, {}
        grp_chain1 = {}
        last_nsq_pe = {0: 0, 1: 0}
        last_nrm_red = 0

        def ln_prep(g):
            gm = g % 2
            sqa = {}
            for tix in range(4):
                tile = 4 * g + tix
                if tile >= 8:
                    W("sync", "dve", app_c[tile - 8])
                x_load[tile] = I("sync", f"ldx{tile % 8}", op(
                    "dma_start", out=x_ring[tile % 8][:],
                    in_=x_seq[tile * 128:(tile + 1) * 128, :]), 16)
                W("scalar", f"ldx{tile % 8}", x_load[tile])
                if g >= 2 and tix == 0:
                    W("scalar", "dve", grp_chain1[g - 2])
                if tile >= 1:
                    W("scalar", "act", sq_act[tile - 1])
                I("scalar", "act", op(
                    "activation", sq_scr[tile % 2][:], x_ring[tile % 8][:],
                    AF.Square, accum_out=stats_sq[gm][:, tix:tix + 1]))
                sqa[tix] = I("scalar", "act", op(
                    "activation", sq_scr[(tile + 1) % 2][:],
                    x_ring[tile % 8][:],
                    AF.Copy, accum_out=stats_su[gm][:, tix:tix + 1]))
                sq_act[tile] = sqa[tix]

            W("vector", "act", sqa[3])
            if g >= 2:
                W("vector", "dve", grp_chain5[g - 2])
            E("vector", op("tensor_scalar_mul", stats_m[gm][:],
                           stats_su[gm][:], 1.0 / C))
            c1 = I("vector", "dve", op("tensor_scalar_mul", stats_v[:],
                                       stats_sq[gm][:], 1.0 / C))
            grp_chain1[g] = c1
            W("vector", "dve", c1)
            c2 = I("vector", "dve", op("scalar_tensor_tensor", stats_t[gm][:],
                                       stats_m[gm][:], -1.0, stats_m[gm][:],
                                       ALU.mult, ALU.mult))
            W("vector", "dve", c2)
            c3 = I("vector", "dve", op("tensor_add", stats_v[:], stats_v[:],
                                       stats_t[gm][:]))
            W("scalar", "dve", c3)
            c4 = I("scalar", "act", op("activation", stats_t[gm][:],
                                       stats_v[:], AF.Sqrt, bias=s_eps[:]))
            grp_c4[g] = c4
            return c4

        ln_prep(0)
        for g in range(NGRP):
            gm = g % 2
            W("vector", "act", grp_c4[g])
            c5 = I("vector", "dve", op("reciprocal", stats_r[gm][:],
                                       stats_t[gm][:]))
            grp_chain5[g] = c5
            if g + 1 < NGRP:
                ln_prep(g + 1)
            for tix in range(4):
                tile = 4 * g + tix
                W("vector", "dve", c5)
                if tile >= 2:
                    W("vector", "pe", tr_c[tile - 2])
                app_c[tile] = I("vector", "dve", op(
                    "tensor_scalar", xln_tm[tile % 2][:], x_ring[tile % 8][:],
                    stats_m[gm][:, tix:tix + 1], stats_r[gm][:, tix:tix + 1],
                    ALU.subtract, ALU.mult))
                for c8 in range(8):
                    i = tile * 8 + c8
                    if i == 0:
                        W("tensor", "ldc", ldc_all)
                    if c8 == 0:
                        W("tensor", "dve", app_c[tile])
                        if tix == 0 and g > 1:
                            W("tensor", "dve", qk_evac[g - 2])
                            W("tensor", "dve", v_evac[(g - 2, 1)])
                    W("tensor", "dve", tr_evac[i - 2])
                    p_tr = I("tensor", "pe", op(
                        "transpose", tr_ps[i % 2][:],
                        xln_tm[tile % 2][:, c8 * 128:(c8 + 1) * 128],
                        s_ident[:]))
                    W("vector", "pe", p_tr)
                    tr_evac[i] = I("vector", "dve", op(
                        "tensor_copy",
                        xln_fm[g % 2][:, c8 * 512 + tix * 128:
                                       c8 * 512 + (tix + 1) * 128],
                        tr_ps[i % 2][:]))
                tr_c[tile] = cnt["pe"]
            evac32 = cnt["dve"]

            W("tensor", "dve", evac32)
            if g == 0:
                W("tensor", "ldwq", ldwq_all)
            if g > 0:
                W("tensor", "dve", qk_evac[g - 1])
            p_qk = None
            for p in range(2):
                for path in range(2):
                    bank = qk_ps[p * 2 + path]
                    ocs = path * 256 + p * 128
                    for icc in range(8):
                        E("tensor", op(
                            "matmul", bank[:],
                            s_wqkvT[:, icc * 768 + ocs: icc * 768 + ocs + 128],
                            xln_fm[g % 2][:, icc * 512:(icc + 1) * 512],
                            start=(icc == 0), stop=False))
                    p_qk = I("tensor", "pe", op(
                        "matmul", bank[:], s_bqkv[:, ocs:ocs + 128],
                        s_ones_b[:], start=False, stop=True))
            W("vector", "pe", p_qk)
            sl = slice(g * 512, (g + 1) * 512)
            for p in range(2):
                E("vector", op("tensor_mul", q_sb[p][:, sl], qk_ps[p * 2][:],
                               f_tab[:, sl]))
                qk_evac[g] = I("vector", "dve", op(
                    "tensor_mul", k_sb[p][:, sl], qk_ps[p * 2 + 1][:],
                    f_tab[:, sl]))

            for half in range(2):
                if half == 0:
                    W("tensor", "dve", evac32)
                    if g > 0:
                        W("tensor", "dve", v_evac[(g - 1, 1)])
                else:
                    W("tensor", "dve", v_evac[(g, 0)])
                p_v = None
                for sub2 in range(2):
                    sub = half * 2 + sub2
                    vslc = v_ps[:, sub2 * 256:(sub2 + 1) * 256]
                    for icc in range(8):
                        E("tensor", op(
                            "matmul", vslc,
                            xln_fm[g % 2][:, icc * 512 + sub * 128:
                                          icc * 512 + (sub + 1) * 128],
                            s_wqkvT[:, icc * 768 + 512: icc * 768 + 768],
                            start=(icc == 0), stop=False))
                    p_v = I("tensor", "pe", op(
                        "matmul", vslc, s_ones_b[0:1, 0:128],
                        s_bqkv[:, 512:768], start=False, stop=True))
                W("vector", "pe", p_v)
                ev = None
                for sub2 in range(2):
                    kblk = g * 4 + half * 2 + sub2
                    for h in range(HPC):
                        ev = I("vector", "dve", op(
                            "tensor_copy",
                            v_sb[h][:, kblk * 65: kblk * 65 + 64],
                            v_ps[:, sub2 * 256 + h * 64:
                                 sub2 * 256 + (h + 1) * 64]))
                v_evac[(g, half)] = ev

            for p in range(2):
                for path in range(2):
                    src = (q_sb if path == 0 else k_sb)[p]
                    nm = path
                    W("scalar", "dve", qk_evac[g])
                    W("scalar", "pe", last_nsq_pe[nm])
                    a_n = I("scalar", "act", op(
                        "activation", s_nsq[nm][:], src[:, sl], AF.Square))
                    W("tensor", "act", a_n)
                    W("tensor", "dve", last_nrm_red)
                    p_n = I("tensor", "pe", op(
                        "matmul", nrm_ps[:], s_ones_c[:], s_nsq[nm][:],
                        start=True, stop=True))
                    last_nsq_pe[nm] = p_n
                    W("vector", "pe", p_n)
                    dst = (s_nq if path == 0 else s_nk)
                    last_nrm_red = I("vector", "dve", op(
                        "reduce_max", dst[0:1, 2 * g + p: 2 * g + p + 1],
                        nrm_ps[:], axis=AX.X))

        W("vector", "dve", last_nrm_red)
        E("vector", op("reduce_max", s_sm[0:1, 0:1], s_nq[:], axis=AX.X))
        sc1 = I("vector", "dve", op("reduce_max", s_sm[0:1, 1:2], s_nk[:],
                                    axis=AX.X))
        W("vector", "dve", sc1)
        sc2 = I("vector", "dve", op("tensor_mul", s_sm[0:1, 2:3],
                                    s_sm[0:1, 0:1], s_sm[0:1, 1:2]))
        W("scalar", "dve", sc2)
        sc3 = I("scalar", "act", op("activation", s_sm[0:1, 3:4],
                                    s_sm[0:1, 2:3], AF.Sqrt))
        W("vector", "act", sc3)
        sc4 = I("vector", "dve", op("tensor_scalar", s_sm[0:1, 4:5],
                                    s_sm[0:1, 3:4], -0.125, SHIFT_MARGIN,
                                    ALU.mult, ALU.add))
        W("vector", "dve", sc4)
        sc5 = I("vector", "dve", op("tensor_scalar_min", s_sm[0:1, 5:6],
                                    s_sm[0:1, 4:5], 0.0))
        W("tensor", "dve", sc5)
        W("tensor", "dve", qk_evac[NGRP - 1])
        p_sh = I("tensor", "pe", op("matmul", qk_ps[0][:, 0:1], s_ones_f[:],
                                    s_sm[0:1, 5:6], start=True, stop=True))
        W("vector", "pe", p_sh)
        I("vector", "dve", op("tensor_copy", s_shiftn[:], qk_ps[0][:, 0:1]))

        ps01.close()
        st0.close()

        ps2 = ExitStack()
        s_cell = [psum(ps2, f"s_cell{i}", [128, 1024]) for i in range(2)]
        y_ps = [psum(ps2, f"y_ps{i}", [65, 512]) for i in range(4)]

        st2 = ExitStack()
        e_cell = [sb(st2, f"e_cell{i}", [128, 1024], BF16) for i in range(3)]
        yev = sb(st2, "yev", [128, 512], F32)
        ynrm = [sb(st2, f"ynrm{i}", [128, 512], BF16) for i in range(2)]
        rcp = sb(st2, "rcp", [65, 512], F32)

        ph1_dve, ph1_pe, ph1_act = cnt["dve"], cnt["pe"], cnt["act"]
        for eng in ("tensor", "scalar", "vector", "gpsimd"):
            W(eng, "dve", ph1_dve)
            W(eng, "pe", ph1_pe)
            W(eng, "act", ph1_act)

        exp_c, attv_c, gate_c = {}, {}, {}
        ser = 0
        ytile = 0
        y_dma, y_mult = {}, {}
        pending_norm = [None]

        QG = NGRP // 4
        w_cast = {}
        for wi, wsrc in enumerate((w_proj, w_fr, w_fk, w_fv)):
            w_cast[wi] = I("gpsimd", "ldw2", op(
                "dma_start", out=w_bf[wi], in_=wsrc), 16)
        for qt in reversed(range(NGRP)):
            for pair in range(2):
                hA, hB = 2 * pair, 2 * pair + 1
                nkb = 4 * (qt + 1)
                qsl = slice(qt * 512, (qt + 1) * 512)
                base = ser

                def scores_emit(j):
                    sj = base + j
                    cell = s_cell[sj % 2]
                    ksl = slice(j * 128, (j + 1) * 128)
                    if sj >= 2:
                        W("tensor", "sE", exp_c[sj - 2])
                    E("tensor", op("matmul", cell[:, 0:512],
                                   k_sb[pair][0:64, ksl],
                                   q_sb[pair][0:64, qsl],
                                   start=True, stop=True))
                    sS_c = I("tensor", "sS", op(
                        "matmul", cell[:, 512:1024], k_sb[pair][64:128, ksl],
                        q_sb[pair][64:128, qsl], start=True, stop=True))
                    W("scalar", "sS", sS_c)
                    if sj >= 3:
                        W("scalar", "sV", attv_c[sj - 3])
                    exp_c[sj] = I("scalar", "sE", op(
                        "activation", e_cell[sj % 3][:], s_cell[sj % 2][:],
                        AF.Exp, scale=0.125, bias=s_shiftn[:]))
                    if j >= nkb - 4:
                        o384 = 384 - 128 * (j - (nkb - 4))
                        ecl = e_cell[sj % 3]
                        W("vector", "sE", exp_c[sj])
                        E("vector", op("tensor_mul", ecl[:, 0:512],
                                       ecl[:, 0:512],
                                       s_maskd[:, o384:o384 + 512]))
                        gate_c[sj] = ("sM", I("vector", "sM", op(
                            "tensor_mul", ecl[:, 512:1024], ecl[:, 512:1024],
                            s_maskd[:, o384:o384 + 512])))
                    else:
                        gate_c[sj] = ("sE", exp_c[sj])

                def attv_emit(j, yset):
                    sj = base + j
                    ecl = e_cell[sj % 3]
                    gs, gv = gate_c[sj]
                    if gs == "sM":
                        W("tensor", "sE", exp_c[sj])
                    W("tensor", gs, gv)
                    if j == 0 and ytile >= 2:
                        W("tensor", "dve", y_mult[ytile - 2])
                    E("tensor", op("matmul", y_ps[2 * yset][:],
                                   v_sb[hA][:, j * 65:(j + 1) * 65],
                                   ecl[:, 0:512], start=(j == 0),
                                   stop=(j == nkb - 1)))
                    attv_c[sj] = I("tensor", "sV", op(
                        "matmul", y_ps[2 * yset + 1][:],
                        v_sb[hB][:, j * 65:(j + 1) * 65],
                        ecl[:, 512:1024], start=(j == 0),
                        stop=(j == nkb - 1)))

                yset = ytile % 2
                scores_emit(0)
                if nkb > 1:
                    scores_emit(1)
                for j in range(nkb):
                    if j + 2 < nkb:
                        scores_emit(j + 2)
                    if j == min(4, nkb - 1) and pending_norm[0] is not None:
                        pending_norm[0]()
                        pending_norm[0] = None
                    attv_emit(j, yset)
                ser += nkb

                W("vector", "sV", attv_c[ser - 1])
                if ytile >= 1:
                    W("vector", "dve", y_mult[ytile - 1])
                yA, yB = y_ps[2 * yset], y_ps[2 * yset + 1]
                E("vector", op("reciprocal", rcp[0:1, :], yA[64:65, :]))
                E("vector", op("reciprocal", rcp[64:65, :], yB[64:65, :]))
                E("vector", op("tensor_copy", yev[0:64, :], yA[0:64, :]))
                d_ev = I("vector", "dve", op(
                    "tensor_copy", yev[64:128, :], yB[0:64, :]))

                def mk_norm(qt=qt, pair=pair, n=ytile, yset=yset, d_ev=d_ev):
                    def norm():
                        yA, yB = y_ps[2 * yset], y_ps[2 * yset + 1]
                        W("tensor", "dve", d_ev)
                        E("tensor", op("matmul", yA[0:64, :],
                                       s_ones_f[0:1, 0:64], rcp[0:1, :],
                                       start=True, stop=True))
                        p_bc = I("tensor", "pe", op(
                            "matmul", yB[0:64, :], s_ones64[64:65, :],
                            rcp[64:65, :], start=True, stop=True))
                        W("vector", "pe", p_bc)
                        if n >= 2:
                            W("vector", f"sty{n % 2}", y_dma[n - 2])
                        yt = ynrm[n % 2]
                        E("vector", op("tensor_mul", yt[0:64, :],
                                       yev[0:64, :], yA[0:64, :]))
                        y_mult[n] = I("vector", "dve", op(
                            "tensor_mul", yt[64:128, :], yev[64:128, :],
                            yB[0:64, :]))
                        W("sync", "dve", y_mult[n])
                        iq = qt // QG
                        y_dma[n] = I("sync", f"sty{n % 2}", op(
                            "dma_start",
                            out=cc_in_q[iq][pair * 128:(pair + 1) * 128,
                                            (qt % QG) * 512:
                                            (qt % QG + 1) * 512],
                            in_=yt[:]), 16)
                    return norm
                pending_norm[0] = mk_norm()
                ytile += 1
            if qt % QG == 0:
                if pending_norm[0] is not None:
                    pending_norm[0]()
                    pending_norm[0] = None
                iq = qt // QG
                W("gpsimd", "sty0", cnt["sty0"])
                W("gpsimd", "sty1", cnt["sty1"])
                I("gpsimd", "scc", op(
                    "collective_compute", "AllGather", ALU.bypass,
                    replica_groups=GROUPS, ins=[cc_in_q[iq]],
                    outs=[cc_out_all[iq * C:(iq + 1) * C, :]]))

        ph2_dve = cnt["dve"]

        stw = ExitStack()
        w_projT = sb(stw, "w_projT", [128, 8 * 1024], BF16, side="right")
        w_frT = sb(stw, "w_frT", [128, 8 * 1024], BF16, side="right")
        w_fkT = sb(stw, "w_fkT", [128, 8 * 1024], BF16, side="right")
        w_fvT = sb(stw, "w_fvT", [128, 8 * 1024], BF16, side="right")

        ps2.close()
        st2.close()
        st1.close()

        st4 = ExitStack()
        y_tile = [sb(st4, f"y_tile{i}", [128, 8 * 512], BF16)
                  for i in range(2)]
        y_halo = sb(st4, "y_halo", [128, 8 * 128], BF16)
        xr_t = [sb(st4, f"xr_t{i}", [128, C], F32) for i in range(2)]
        out1 = [sb(st4, f"out1{i}", [128, 4 * C], F32) for i in range(2)]
        h2_tm = [sb(st4, f"h2_tm{i}", [128, C], BF16) for i in range(2)]
        h2_fm = sb(st4, "h2_fm", [128, 8 * 513], BF16)
        hstage = sb(st4, "hstage", [128, 8], BF16)
        p4sq = sb(st4, "p4sq", [128, C], BF16)
        p4st = [sb(st4, f"p4st{i}", [128, 5], F32) for i in range(7)]
        d_mix = sb(st4, "d_mix", [128, 8 * 512], BF16)
        xr_fm = sb(st4, "xr_fm", [128, 8 * 512], BF16)
        xk_fm = sb(st4, "xk_fm", [128, 8 * 512], BF16)
        kact = sb(st4, "kact", [128, 8 * 512], BF16)
        ksc = sb(st4, "ksc", [128, 512], F32)
        r_f32 = sb(st4, "r_f32", [128, C], F32)
        fin = [sb(st4, f"fin{i}", [128, C], F32) for i in range(3)]

        ps4 = ExitStack()
        pr_ps = psum(ps4, "pr_ps", [128, 1024])
        t4_ps = [psum(ps4, f"t4_ps{i}", [128, 128], BF16) for i in range(2)]
        rr_ps = psum(ps4, "rr_ps", [128, 1024])
        kv_ps = psum(ps4, "kv_ps", [128, 1024])

        tr4_evac = {-2: 0, -1: 0}
        t4i_box = [0]
        W("scalar", "ldw2", cnt["ldw2"])
        for wi, wdst in enumerate((w_projT, w_frT, w_fkT, w_fvT)):
            for icc in range(8):
                I("scalar", "ldwt", op(
                    "dma_start", out=wdst[:, icc * 1024:(icc + 1) * 1024],
                    in_=w_bf[wi][:, icc * 128:(icc + 1) * 128],
                    transpose=True), 16)
        ldwt_all = cnt["ldwt"]

        def sy_pid(e):
            qv = e.partition_id() % 4
            sy_pid.qrow = e.snap(qv * 8, donate=True, min_val=0, max_val=32)
            sy_pid.hrow = e.snap(((qv + 3) % 4) * 8, donate=True,
                                 min_val=0, max_val=32)
        E("sync", sy_pid)
        cc_view = cc_out_all.rearrange("(r b) t -> b r t", b=128)

        resid_d, xres_ld = {}, {}
        p4sq_act = []
        tr4_done = {}
        last_apply_box = [0]

        def ln2_chain(cols, prev_apply):
            W("vector", "dve", prev_apply)
            E("vector", op("tensor_scalar_mul", p4st[2][:, cols],
                           p4st[0][:, cols], 1.0 / C))
            k1 = I("vector", "dve", op("tensor_scalar_mul", p4st[3][:, cols],
                                       p4st[1][:, cols], 1.0 / C))
            W("vector", "dve", k1)
            k2 = I("vector", "dve", op("scalar_tensor_tensor", p4st[4][:, cols],
                                       p4st[2][:, cols], -1.0, p4st[2][:, cols],
                                       ALU.mult, ALU.mult))
            W("vector", "dve", k2)
            k3 = I("vector", "dve", op("tensor_add", p4st[3][:, cols],
                                       p4st[3][:, cols], p4st[4][:, cols]))
            W("scalar", "dve", k3)
            k4 = I("scalar", "act", op("activation", p4st[4][:, cols],
                                       p4st[3][:, cols], AF.Sqrt,
                                       bias=s_eps[:]))
            W("vector", "act", k4)
            k5 = I("vector", "dve", op("reciprocal", p4st[5][:, cols],
                                       p4st[4][:, cols]))
            return k5

        def emit_proj(tile, lhs_fn, stcol, o1, pe_waits, act_waits=()):
            if tile >= 2:
                W("sync", "dve", resid_d[tile - 2])
            xres_ld[tile] = I("sync", f"ldr{tile % 2}", op(
                "dma_start", out=xr_t[tile % 2][:],
                in_=x_res[tile * 128:(tile + 1) * 128, :]), 16)
            for sm, vv in pe_waits:
                W("tensor", sm, vv)
            if tile >= 1:
                W("tensor", "dve", resid_d.get(tile - 1, 0))
            p_pr = None
            for half in range(2):
                o = pr_ps[:, half * 512:(half + 1) * 512]
                for c8 in range(8):
                    E("tensor", op("matmul", o, lhs_fn(c8),
                                   w_projT[:, c8 * 1024 + half * 512:
                                           c8 * 1024 + half * 512 + 512],
                                   start=(c8 == 0), stop=False))
                p_pr = I("tensor", "pe", op(
                    "matmul", o, s_ones_b[0:1, 0:128],
                    s_bproj[:, half * 512:half * 512 + 512],
                    start=False, stop=True))
            W("vector", "pe", p_pr)
            W("vector", f"ldr{tile % 2}", xres_ld[tile])
            for sm, vv in act_waits:
                W("vector", sm, vv)
            E("vector", op("tensor_add", o1[:, 0:512], pr_ps[:, 0:512],
                           xr_t[tile % 2][:, 0:512]))
            resid_d[tile] = I("vector", "dve", op(
                "tensor_add", o1[:, 512:1024], pr_ps[:, 512:1024],
                xr_t[tile % 2][:, 512:1024]))
            W("scalar", "dve", resid_d[tile])
            if p4sq_act:
                W("scalar", "act", p4sq_act[-1])
            a_sq = I("scalar", "act", op(
                "activation", p4sq[:], o1[:], AF.Square,
                accum_out=p4st[1][:, stcol:stcol + 1]))
            p4sq_act.append(a_sq)
            W("vector", "dve", resid_d[tile])
            d_rs = I("vector", "dve", op(
                "reduce_sum", p4st[0][:, stcol:stcol + 1], o1[:], axis=AX.X))
            return a_sq, d_rs

        def emit_apply_tr(tile, stcol, o1, halo):
            W("vector", "pe", tr4_done.get(tile - 2, 0))
            a_ap = I("vector", "dve", op(
                "tensor_scalar", h2_tm[tile % 2][:], o1[:],
                p4st[2][:, stcol:stcol + 1], p4st[5][:, stcol:stcol + 1],
                ALU.subtract, ALU.mult))
            last_apply_box[0] = a_ap
            for c8 in range(8):
                t4i = t4i_box[0]
                if c8 == 0:
                    W("tensor", "dve", a_ap)
                W("tensor", "dve", tr4_evac[t4i - 2])
                p_tr = I("tensor", "pe", op(
                    "transpose", t4_ps[t4i % 2][:],
                    h2_tm[tile % 2][:, c8 * 128:(c8 + 1) * 128], s_ident[:]))
                W("vector", "pe", p_tr)
                if halo:
                    tr4_evac[t4i] = I("vector", "dve", op(
                        "tensor_scalar_mul", h2_fm[:, c8 * 513:c8 * 513 + 1],
                        t4_ps[t4i % 2][:, 127:128], s_halo[:, 0:1]))
                else:
                    sub = (tile - 1) % 4
                    tr4_evac[t4i] = I("vector", "dve", op(
                        "tensor_copy",
                        h2_fm[:, c8 * 513 + 1 + sub * 128:
                              c8 * 513 + 1 + (sub + 1) * 128],
                        t4_ps[t4i % 2][:]))
                t4i_box[0] += 1
            tr4_done[tile] = cnt["pe"]

        W("sync", "scc", 4)
        yhl = I("sync", "ldyh", (
            lambda e: e.dma_start(
                out=y_halo[:].rearrange("p (a t) -> p a t", t=128),
                in_=cc_view[:, bass.ds(sy_pid.hrow, 8),
                            QSPAN - 128:QSPAN])), 8 * 16)
        out1h = out1[1][:, 0:C]
        a_sq, d_rs = emit_proj(
            0, lambda c8: y_halo[:, c8 * 128:(c8 + 1) * 128], 4, out1h,
            pe_waits=(("ldwt", ldwt_all), ("ldyh", yhl)))
        W("vector", "act", a_sq)
        W("vector", "dve", d_rs)
        ch = ln2_chain(slice(4, 5), 0)
        W("vector", "dve", ch)
        emit_apply_tr(0, 4, out1h, halo=True)
        halo_sq_act = a_sq
        halo_apply = last_apply_box[0]

        proj_pe_done = {}
        f2_prev = [0]
        mix_done_of = {}
        stripe_f2 = {}
        stripe_pe_end = 0
        kact_act = {}
        last_sig = {}
        fin_store = {}
        fin_m = {}

        def emit_ffn(sf):
            nonlocal stripe_pe_end, mix_done_of
            o1 = out1[sf % 2]
            for occ in range(8):
                par = occ % 2
                kslc = kv_ps[:, par * 512:(par + 1) * 512]
                if occ == 0:
                    W("tensor", "dve", mix_done_of[sf])
                if occ >= 2:
                    W("tensor", "act", kact_act.get((sf, occ - 2), 0))
                p_k = None
                for icc in range(8):
                    f = op("matmul", kslc,
                           w_fkT[:, icc * 1024 + occ * 128:
                                 icc * 1024 + (occ + 1) * 128],
                           xk_fm[:, icc * 512:(icc + 1) * 512],
                           start=(icc == 0), stop=(icc == 7))
                    if icc == 7:
                        p_k = I("tensor", "pe", f)
                    else:
                        E("tensor", f)
                W("vector", "pe", p_k)
                if occ >= 1:
                    W("vector", "act", kact_act[(sf, occ - 1)])
                elif sf > 0:
                    W("vector", "act", kact_act[(sf - 1, 7)])
                d_rl = I("vector", "dve", op("tensor_scalar_max", ksc[:],
                                             kslc, 0.0))
                W("scalar", "dve", d_rl)
                if sf > 0 and occ == 0:
                    W("scalar", "pe", stripe_pe_end)
                kact_act[(sf, occ)] = I("scalar", "act", op(
                    "activation", kact[:, occ * 512:(occ + 1) * 512], ksc[:],
                    AF.Square))

            for sub in range(4):
                idx = sf * 4 + sub
                if sub == 0:
                    W("tensor", "dve", mix_done_of[sf])
                else:
                    W("tensor", "act", last_sig[idx - 1])
                p_r = None
                for half in range(2):
                    o = rr_ps[:, half * 512:(half + 1) * 512]
                    for c8 in range(8):
                        f = op("matmul", o,
                               xr_fm[:, c8 * 512 + sub * 128:
                                     c8 * 512 + (sub + 1) * 128],
                               w_frT[:, c8 * 1024 + half * 512:
                                     c8 * 1024 + half * 512 + 512],
                               start=(c8 == 0), stop=(c8 == 7))
                        if half == 1 and c8 == 7:
                            p_r = I("tensor", "pe", f)
                        else:
                            E("tensor", f)
                W("scalar", "pe", p_r)
                if idx >= 1:
                    W("scalar", "dve", fin_m[idx - 1])
                last_sig[idx] = I("scalar", "act", op(
                    "activation", r_f32[:], rr_ps[:], AF.Sigmoid))
                if sub == 0:
                    W("tensor", "act", kact_act[(sf, 7)])
                else:
                    W("tensor", "dve", fin_m[idx - 1])
                p_v = None
                for half in range(2):
                    o = kv_ps[:, half * 512:(half + 1) * 512]
                    for c8 in range(8):
                        f = op("matmul", o,
                               kact[:, c8 * 512 + sub * 128:
                                    c8 * 512 + (sub + 1) * 128],
                               w_fvT[:, c8 * 1024 + half * 512:
                                     c8 * 1024 + half * 512 + 512],
                               start=(c8 == 0), stop=(c8 == 7))
                        if half == 1 and c8 == 7:
                            p_v = I("tensor", "pe", f)
                        else:
                            E("tensor", f)
                W("vector", "pe", p_v)
                W("vector", "act", last_sig[idx])
                if f2_prev[0]:
                    W("vector", "dve", f2_prev[0])
                E("vector", op("tensor_mul", fin[0][:, 0:512],
                               kv_ps[:, 0:512], r_f32[:, 0:512]))
                fin_m[idx] = I("vector", "dve", op(
                    "tensor_mul", fin[0][:, 512:1024], kv_ps[:, 512:1024],
                    r_f32[:, 512:1024]))
                W("vector", "dve", fin_m[idx])
                if idx >= 2:
                    W("vector", f"sto{idx % 2}", fin_store[idx - 2])
                fb = fin[1 + idx % 2]
                E("vector", op("tensor_add", fb[:, 0:512], fin[0][:, 0:512],
                               o1[:, sub * 1024:sub * 1024 + 512]))
                f2 = I("vector", "dve", op(
                    "tensor_add", fb[:, 512:1024], fin[0][:, 512:1024],
                    o1[:, sub * 1024 + 512:(sub + 1) * 1024]))
                f2_prev[0] = f2
                W("sync", "dve", f2)
                fin_store[idx] = I("sync", f"sto{idx % 2}", op(
                    "dma_start",
                    out=out_t[idx * 128:(idx + 1) * 128, :], in_=fb[:]), 16)
            stripe_pe_end = cnt["pe"]
            stripe_f2[sf] = f2_prev[0]

        for s in range(NSTR):
            if s >= 2:
                W("sync", "pe", proj_pe_done[s - 2])
            ylast = I("sync", "ldy", (
                lambda e, s=s: e.dma_start(
                    out=y_tile[s % 2][:].rearrange("p (a t) -> p a t", t=512),
                    in_=cc_view[:, bass.ds(sy_pid.qrow, 8),
                                s * 512:(s + 1) * 512])), 8 * 16)

            st_list = []
            for sub in range(4):
                tile = 1 + 4 * s + sub
                pw = [("ldy", ylast)] if sub == 0 else []
                aw = []
                if s >= 2 and sub == 0:
                    aw = [("dve", stripe_f2[s - 2])]
                a_sq, d_rs = emit_proj(
                    tile,
                    (lambda c8, sub=sub, s=s:
                     y_tile[s % 2][:, c8 * 512 + sub * 128:
                                   c8 * 512 + (sub + 1) * 128]),
                    sub, out1[s % 2][:, sub * 1024:(sub + 1) * 1024],
                    pe_waits=pw, act_waits=aw)
                st_list.append((a_sq, d_rs))
            proj_pe_done[s] = cnt["pe"]
            W("vector", "act", st_list[-1][0])
            W("vector", "dve", st_list[-1][1])
            ch = ln2_chain(slice(0, 4), last_apply_box[0])
            W("vector", "dve", ch)

            if s > 0:
                emit_ffn(s - 1)
                W("vector", "dve", mix_done_of[s - 1])
                hv = h2_fm[:].rearrange("p (c w) -> p c w", w=513)
                b0 = I("vector", "dve", op("tensor_copy", hstage[:],
                                           hv[:, :, 512]))
                W("vector", "dve", b0)
                b1 = I("vector", "dve", op("tensor_copy", hv[:, :, 0],
                                           hstage[:]))
                W("vector", "dve", b1)

            for sub in range(4):
                tile = 1 + 4 * s + sub
                emit_apply_tr(tile, sub,
                              out1[s % 2][:, sub * 1024:(sub + 1) * 1024],
                              halo=False)
            evac_done = cnt["dve"]

            W("vector", "dve", evac_done)
            W("vector", "ldmu", ldmu_all)
            if s > 0:
                W("vector", "pe", stripe_pe_end)
            dm = None
            for c8 in range(8):
                cur = h2_fm[:, c8 * 513 + 1: c8 * 513 + 513]
                prv = h2_fm[:, c8 * 513: c8 * 513 + 512]
                dm = I("vector", "dve", op(
                    "tensor_sub", d_mix[:, c8 * 512:(c8 + 1) * 512], cur, prv))
            W("vector", "dve", dm)
            mx = None
            for c8 in range(8):
                prv = h2_fm[:, c8 * 513: c8 * 513 + 512]
                E("vector", op("scalar_tensor_tensor",
                               xr_fm[:, c8 * 512:(c8 + 1) * 512],
                               d_mix[:, c8 * 512:(c8 + 1) * 512],
                               s_mur[:, c8:c8 + 1], prv, ALU.mult, ALU.add))
                mx = I("vector", "dve", op(
                    "scalar_tensor_tensor", xk_fm[:, c8 * 512:(c8 + 1) * 512],
                    d_mix[:, c8 * 512:(c8 + 1) * 512],
                    s_muk[:, c8:c8 + 1], prv, ALU.mult, ALU.add))
            mix_done_of[s] = mx
        emit_ffn(NSTR - 1)

        W("sync", "sto0", cnt["sto0"])
        W("sync", "sto1", cnt["sto1"])
        W("gpsimd", "sto0", cnt["sto0"])
        W("gpsimd", "sto1", cnt["sto1"])

        for eng in ENGS:
            def body(e, ops=tuple(q[eng]), eng=eng):
                for fi, f in enumerate(ops):
                    try:
                        f(e)
                    except Exception as ex:
                        import traceback
                        traceback.print_exc()
                        raise RuntimeError(
                            f"flush fail {eng}[{fi}]: "
                            f"{getattr(f, 'opname', None) or [getattr(d, 'opname', None) for d in (f.__defaults__ or [])]}"
                        ) from ex
            getattr(block, eng)(body)

        ps4.close()
        st4.close()
        stw.close()

    return nc



def _mask_host():
    p = np.arange(128)[:, None]
    g = np.arange(896)[None, :]
    return (p <= g - 384).astype(ml_dtypes.bfloat16)


def prep_inputs(inp, T):
    x = np.ascontiguousarray(np.asarray(inp["L0_attn_x"], dtype=np.float32))
    W_attn = np.asarray(inp["L0_attn_W_attn"], dtype=np.float32)
    b_attn = np.asarray(inp["L0_attn_b_attn"], dtype=np.float32)
    TSL = T // 4
    shared = dict(
        w_proj=np.ascontiguousarray(np.asarray(inp["L0_attn_W_proj"], np.float32)),
        b_proj=np.asarray(inp["L0_attn_b_proj"], np.float32).reshape(1, C),
        w_fr=np.ascontiguousarray(np.asarray(inp["L0_ffn_W_r"], np.float32)),
        w_fk=np.ascontiguousarray(np.asarray(inp["L0_ffn_W_k"], np.float32)),
        w_fv=np.ascontiguousarray(np.asarray(inp["L0_ffn_W_v"], np.float32)),
        mu_r=np.asarray(inp["L0_ffn_mu_r"], np.float32),
        mu_k=np.asarray(inp["L0_ffn_mu_k"], np.float32),
        identb=np.eye(128, dtype=ml_dtypes.bfloat16),
        maskd=_mask_host(),
    )
    in_maps = []
    for c in range(8):
        seq, qq = c // 4, c % 4
        heads = [4 * qq + h for h in range(HPC)]
        rows, brows = [], []
        for sec in range(3):
            for h in heads:
                rows.append(W_attn[sec * C + h * HD: sec * C + (h + 1) * HD])
                brows.append(b_attn[sec * C + h * HD: sec * C + (h + 1) * HD])
        xs = x[seq, :T]
        t0 = qq * TSL
        halo_rows = (np.zeros((128, C), np.float32) if qq == 0
                     else xs[t0 - 128: t0])
        in_maps.append(dict(
            shared,
            x_seq=np.ascontiguousarray(xs),
            x_res=np.ascontiguousarray(
                np.concatenate([halo_rows, xs[t0: t0 + TSL]], axis=0)),
            w_qkv=np.ascontiguousarray(np.concatenate(rows, axis=0)),
            b_qkv=np.concatenate(brows).reshape(1, 768),
            halo_s=np.full((128, 1), 0.0 if qq == 0 else 1.0, np.float32),
        ))
    return in_maps


_NC_CACHE = {}


def kernel(**inputs):
    T = 8192
    if T not in _NC_CACHE:
        _NC_CACHE[T] = build(T)
    nc = _NC_CACHE[T]
    in_maps = prep_inputs(inputs, T)
    res = run_bass_kernel_spmd(nc, in_maps, core_ids=list(range(8)))
    TSL = T // 4
    outp = np.empty((2, T, C), dtype=np.float32)
    for c in range(8):
        seq, qq = c // 4, c % 4
        outp[seq, qq * TSL:(qq + 1) * TSL] = res.results[c]["out"]
    return outp


# version 07 (final)
# speedup vs baseline: 1.2545x; 1.0810x over previous
import math
from contextlib import ExitStack

import numpy as np
import ml_dtypes

import concourse.bass as bass
import concourse.mybir as mybir
from concourse.bass_utils import run_bass_kernel_spmd

F32 = mybir.dt.float32
BF16 = mybir.dt.bfloat16
I32 = mybir.dt.int32
AF = mybir.ActivationFunctionType
ALU = mybir.AluOpType
AX = mybir.AxisListType

C = 1024
HD = 64
HPC = 4
GROUPS = [[0, 1, 2, 3], [4, 5, 6, 7]]
LN_EPS = 1e-5
SHIFT_MARGIN = 60.0
RT2 = math.sqrt(2.0)
TWO_PI = 2.0 * math.pi
ENGS = ("sync", "gpsimd", "tensor", "scalar", "vector")


def op(name, *args, **kw):
    f = lambda e: getattr(e, name)(*args, **kw)
    f.opname = name
    f.opargs = args
    return f


def build(T=8192):
    assert T % 2048 == 0
    TSL = T // 4
    NGRP = T // 512
    NTILP = TSL // 128
    NSTR = TSL // 512

    nc = bass.Bass("TRN2", target_bir_lowering=False)

    x_seq = nc.dram_tensor("x_seq", [T, C], F32, kind="ExternalInput").ap()
    x_res = nc.dram_tensor("x_res", [TSL + 128, C], F32, kind="ExternalInput").ap()
    w_qkv = nc.dram_tensor("w_qkv", [768, C], F32, kind="ExternalInput").ap()
    b_qkv = nc.dram_tensor("b_qkv", [1, 768], F32, kind="ExternalInput").ap()
    w_proj = nc.dram_tensor("w_proj", [C, C], F32, kind="ExternalInput").ap()
    b_proj = nc.dram_tensor("b_proj", [1, C], F32, kind="ExternalInput").ap()
    w_fr = nc.dram_tensor("w_fr", [C, C], F32, kind="ExternalInput").ap()
    w_fk = nc.dram_tensor("w_fk", [C, C], F32, kind="ExternalInput").ap()
    w_fv = nc.dram_tensor("w_fv", [C, C], F32, kind="ExternalInput").ap()
    mu_r = nc.dram_tensor("mu_r", [C], F32, kind="ExternalInput").ap()
    mu_k = nc.dram_tensor("mu_k", [C], F32, kind="ExternalInput").ap()
    identb = nc.dram_tensor("identb", [128, 128], BF16, kind="ExternalInput").ap()
    maskd = nc.dram_tensor("maskd", [128, 896], BF16, kind="ExternalInput").ap()
    halo_s = nc.dram_tensor("halo_s", [128, 1], F32, kind="ExternalInput").ap()
    out_t = nc.dram_tensor("out", [TSL, C], F32, kind="ExternalOutput").ap()

    QSPAN = T // 4
    cc_in_q = [nc.dram_tensor(f"cc_in{i}", [HPC * HD, QSPAN], BF16,
                              kind="Internal").ap() for i in range(4)]
    cc_out_all = nc.dram_tensor("cc_out_all", [4 * C, QSPAN], BF16,
                                kind="Internal").ap()
    w_bf = [nc.dram_tensor(f"w_bf{i}", [C, C], BF16, kind="Internal").ap()
            for i in range(4)]
    wq_bf = nc.dram_tensor("wq_bf", [768, C], BF16, kind="Internal").ap()

    q = {e: [] for e in ENGS}
    sems = {}
    cnt = {}

    def E(eng, f):
        q[eng].append(f)

    def W(eng, sem, val):
        if val and val > 0:
            q[eng].append(lambda e, s=sems[sem], v=val: e.wait_ge(s, v))

    def I(eng, sem, f, k=1):
        cnt[sem] += k
        q[eng].append(lambda e, f=f, s=sems[sem], kk=k: f(e).then_inc(s, kk))
        return cnt[sem]

    ctx = ExitStack()
    with ctx:
        def sb(st, name, shape, dt, side="left"):
            return st.enter_context(nc.sbuf_tensor(name, shape, dt, side=side))

        def psum(st, name, shape, dt=F32):
            return st.enter_context(nc.psum_tensor(name, shape, dt))

        for s in ("ldc", "ldx0", "ldx1", "ldx2", "ldx3", "ldx4", "ldx5",
                  "ldx6", "ldx7", "ldw", "ldwq",
                  "ldmu", "ldw2", "gp", "pe", "act", "dve", "sS", "sE", "sM",
                  "sV",
                  "sty0", "sty1", "scc", "ldyh", "ldy", "ldwt", "ldr0",
                  "ldr1",
                  "sto0", "sto1"):
            sems[s] = ctx.enter_context(nc.semaphore(s))
            cnt[s] = 0

        s_ident = sb(ctx, "s_ident", [128, 128], BF16)
        s_maskd = sb(ctx, "s_maskd", [128, 896], BF16)
        s_ones_b = sb(ctx, "s_ones_b", [1, 512], BF16)
        s_ones_c = sb(ctx, "s_ones_c", [128, 1], BF16)
        s_ones64 = sb(ctx, "s_ones64", [65, 64], F32)
        s_ones_f = sb(ctx, "s_ones_f", [1, 128], F32)
        s_bqkv = sb(ctx, "s_bqkv", [1, 768], BF16)
        s_bproj = sb(ctx, "s_bproj", [1, C], BF16)
        s_mur = sb(ctx, "s_mur", [128, 8], F32)
        s_muk = sb(ctx, "s_muk", [128, 8], F32)
        s_halo = sb(ctx, "s_halo", [128, 1], F32)
        s_shiftn = sb(ctx, "s_shiftn", [128, 1], F32)
        s_eps = sb(ctx, "s_eps", [128, 1], F32)
        s_b8 = sb(ctx, "s_b8", [64, 1], F32)
        s_nq = sb(ctx, "s_nq", [1, 2 * NGRP], F32)
        s_nk = sb(ctx, "s_nk", [1, 2 * NGRP], F32)
        s_sm = sb(ctx, "s_sm", [1, 8], F32)

        block = ctx.enter_context(nc.Block())

        st1 = ExitStack()
        q_sb = [sb(st1, f"q_sb{p}", [128, T], BF16) for p in range(2)]
        k_sb = [sb(st1, f"k_sb{p}", [128, T], BF16) for p in range(2)]
        v_sb = [sb(st1, f"v_sb{h}", [128, (T // 128) * 65], BF16)
                for h in range(HPC)]

        st0 = ExitStack()
        s_wqkvT = sb(st0, "s_wqkvT", [128, 8 * 768], BF16)
        f_tab = sb(st0, "f_tab", [128, T], BF16, side="right")
        s_theta = sb(st0, "s_theta", [1, 64], F32)
        s_seq = sb(st0, "s_seq", [1, 256], F32)
        s_u = sb(st0, "s_u", [64, 256], F32)
        s_kf = sb(st0, "s_kf", [64, 256], F32)
        s_d0 = sb(st0, "s_d0", [64, 256], F32)
        x_ring = [sb(st0, f"x_ring{i}", [128, C], F32) for i in range(8)]
        sq_scr = [sb(st0, f"sq_scr{i}", [128, C], BF16) for i in range(2)]
        xln_tm = [sb(st0, f"xln_tm{i}", [128, C], BF16) for i in range(4)]
        xln_fm = [sb(st0, f"xln_fm{i}", [128, 8 * 512], BF16)
                  for i in range(2)]
        stats_sq = [sb(st0, f"stats_sq{i}", [128, 4], F32) for i in range(2)]
        stats_su = [sb(st0, f"stats_su{i}", [128, 4], F32) for i in range(2)]
        stats_m = [sb(st0, f"stats_m{i}", [128, 4], F32) for i in range(2)]
        stats_v = sb(st0, "stats_v", [128, 4], F32)
        stats_t = [sb(st0, f"stats_t{i}", [128, 4], F32) for i in range(2)]
        stats_r = [sb(st0, f"stats_r{i}", [128, 4], F32) for i in range(2)]
        s_nsq = [sb(st0, f"s_nsq{i}", [128, 512], BF16) for i in range(2)]

        ps01 = ExitStack()
        tr_ps = [psum(ps01, f"tr_ps{i}", [128, 512], BF16) for i in range(2)]
        qk_ps = [psum(ps01, f"qk_ps{i}", [128, 512]) for i in range(4)]
        v_ps = psum(ps01, "v_ps", [128, 512])
        nrm_ps = psum(ps01, "nrm_ps", [1, 512])

        for tns, val in ((s_ones_b, 1.0), (s_ones_c, 1.0), (s_ones_f, 1.0),
                         (s_ones64, 1.0),
                         (s_eps, LN_EPS), (s_b8, 0.125), (s_sm, 0.0)):
            E("gpsimd", op("memset", tns[:], val))
        for h in range(HPC):
            E("gpsimd", op("memset", v_sb[h][:], 1.0))
        gp0 = I("gpsimd", "gp", op("iota", s_theta[:, 0:32], [[1, 32]],
                                   base=0, channel_multiplier=0,
                                   allow_small_or_imprecise_dtypes=True))
        I("gpsimd", "ldw", op("dma_start", out=s_bqkv[:], in_=b_qkv), 16)
        I("gpsimd", "ldw", op("dma_start", out=s_bproj[:], in_=b_proj), 16)
        ldw_wq = I("gpsimd", "ldw", op(
            "dma_start", out=wq_bf, in_=w_qkv), 16)
        for c8 in range(8):
            I("gpsimd", "ldmu", op("dma_start", out=s_mur[:, c8:c8 + 1],
                                   in_=mu_r[c8 * 128:(c8 + 1) * 128],
                                   allow_slow_non_contiguous=True), 16)
            I("gpsimd", "ldmu", op("dma_start", out=s_muk[:, c8:c8 + 1],
                                   in_=mu_k[c8 * 128:(c8 + 1) * 128],
                                   allow_slow_non_contiguous=True), 16)
        ldmu_all = cnt["ldmu"]

        I("sync", "ldc", op("dma_start", out=s_ident[:], in_=identb), 16)
        I("sync", "ldc", op("dma_start", out=s_maskd[:], in_=maskd), 16)
        ldc_all = I("sync", "ldc", op("dma_start", out=s_halo[:], in_=halo_s), 16)

        W("scalar", "gp", gp0)
        a_th = I("scalar", "act", op("activation", s_theta[:, 0:32],
                                     s_theta[:, 0:32], AF.Exp,
                                     scale=-math.log(10000.0) / 32.0))
        W("vector", "act", a_th)
        d_th = I("vector", "dve", op("tensor_copy", s_theta[:, 32:64],
                                     s_theta[:, 0:32]))

        tr_evac = {-2: 0, -1: 0}
        W("scalar", "ldw", cnt["ldw"])
        for icc in range(8):
            I("scalar", "ldwq", op(
                "dma_start", out=s_wqkvT[:, icc * 768:(icc + 1) * 768],
                in_=wq_bf[:, icc * 128:(icc + 1) * 128], transpose=True), 16)
        ldwq_all = cnt["ldwq"]

        a_u_prev = 0
        p_ang_prev = 0
        for g5 in range(T // 256):
            sl = slice(g5 * 256, (g5 + 1) * 256)
            if g5 == 0:
                W("tensor", "dve", d_th)
                W("tensor", "gp", gp0)
            else:
                W("tensor", "dve", a_u_prev)
            W("gpsimd", "pe", p_ang_prev)
            sq_i = I("gpsimd", "gp", op("iota", s_seq[:], [[1, 256]],
                                        base=g5 * 256 + 1,
                                        channel_multiplier=0,
                                        allow_small_or_imprecise_dtypes=True))
            W("tensor", "gp", sq_i)
            p_ang = I("tensor", "pe", op("matmul", qk_ps[0][0:64, 0:256],
                                         s_theta[:], s_seq[:],
                                         start=True, stop=True))
            p_ang_prev = p_ang
            W("vector", "pe", p_ang)
            a_u = I("vector", "dve", op("tensor_scalar", s_u[:],
                                        qk_ps[0][0:64, 0:256],
                                        1.0 / TWO_PI, 0.125,
                                        ALU.mult, ALU.add))
            a_u_prev = a_u
            W("vector", "dve", a_u)
            d1 = I("vector", "dve", op("tensor_scalar", s_kf[:], s_u[:],
                                       8388608.0, 8388608.0,
                                       ALU.add, ALU.subtract))
            W("vector", "dve", d1)
            d2 = I("vector", "dve", op("tensor_sub", s_d0[:], s_u[:], s_kf[:]))
            W("scalar", "dve", d2)
            a_s = I("scalar", "act", op("activation", s_u[:], s_d0[:], AF.Sin,
                                        scale=TWO_PI))
            W("vector", "act", a_s)
            d3 = I("vector", "dve", op("tensor_scalar_mul", f_tab[0:64, sl],
                                       s_u[:], RT2))
            W("vector", "dve", d3)
            I("vector", "dve", op("tensor_copy", f_tab[64:128, sl],
                                  f_tab[0:64, sl]))

        x_load, app_c, trb_evac, tr_grp = {}, {}, {}, {}
        sq_act = {}
        grp_chain5 = {}
        grp_c4 = {}
        qk_evac, v_evac = {}, {}
        grp_chain1 = {}
        last_nsq_pe = {0: 0, 1: 0}
        last_nrm_red = 0

        def ln_prep(g):
            gm = g % 2
            sqa = {}
            for tix in range(4):
                tile = 4 * g + tix
                if tile >= 8:
                    W("sync", "dve", app_c[tile - 8])
                x_load[tile] = I("sync", f"ldx{tile % 8}", op(
                    "dma_start", out=x_ring[tile % 8][:],
                    in_=x_seq[tile * 128:(tile + 1) * 128, :]), 16)
                W("scalar", f"ldx{tile % 8}", x_load[tile])
                if g >= 2 and tix == 0:
                    W("scalar", "dve", grp_chain1[g - 2])
                if tile >= 1:
                    W("scalar", "act", sq_act[tile - 1])
                I("scalar", "act", op(
                    "activation", sq_scr[tile % 2][:], x_ring[tile % 8][:],
                    AF.Square, accum_out=stats_sq[gm][:, tix:tix + 1]))
                sqa[tix] = I("scalar", "act", op(
                    "activation", sq_scr[(tile + 1) % 2][:],
                    x_ring[tile % 8][:],
                    AF.Copy, accum_out=stats_su[gm][:, tix:tix + 1]))
                sq_act[tile] = sqa[tix]

            W("vector", "act", sqa[3])
            if g >= 2:
                W("vector", "dve", grp_chain5[g - 2])
            E("vector", op("tensor_scalar_mul", stats_m[gm][:],
                           stats_su[gm][:], 1.0 / C))
            c1 = I("vector", "dve", op("tensor_scalar_mul", stats_v[:],
                                       stats_sq[gm][:], 1.0 / C))
            grp_chain1[g] = c1
            W("vector", "dve", c1)
            c2 = I("vector", "dve", op("scalar_tensor_tensor", stats_t[gm][:],
                                       stats_m[gm][:], -1.0, stats_m[gm][:],
                                       ALU.mult, ALU.mult))
            W("vector", "dve", c2)
            c3 = I("vector", "dve", op("tensor_add", stats_v[:], stats_v[:],
                                       stats_t[gm][:]))
            W("scalar", "dve", c3)
            c4 = I("scalar", "act", op("activation", stats_t[gm][:],
                                       stats_v[:], AF.Sqrt, bias=s_eps[:]))
            grp_c4[g] = c4
            return c4

        ln_prep(0)
        for g in range(NGRP):
            gm = g % 2
            W("vector", "act", grp_c4[g])
            c5 = I("vector", "dve", op("reciprocal", stats_r[gm][:],
                                       stats_t[gm][:]))
            grp_chain5[g] = c5
            if g + 1 < NGRP:
                ln_prep(g + 1)
            for tix in range(4):
                tile = 4 * g + tix
                W("vector", "dve", c5)
                if g >= 1 and tix == 0:
                    W("vector", "pe", tr_grp[g - 1])
                app_c[tile] = I("vector", "dve", op(
                    "tensor_scalar", xln_tm[tile % 4][:], x_ring[tile % 8][:],
                    stats_m[gm][:, tix:tix + 1], stats_r[gm][:, tix:tix + 1],
                    ALU.subtract, ALU.mult))

            def qkv_icc(j):
                W("tensor", "dve", trb_evac[g * 8 + j])
                if j == 0:
                    if g == 0:
                        W("tensor", "ldwq", ldwq_all)
                    else:
                        W("tensor", "dve", qk_evac[g - 1])
                for p in range(2):
                    for path in range(2):
                        bank = qk_ps[p * 2 + path]
                        ocs = path * 256 + p * 128
                        E("tensor", op(
                            "matmul", bank[:],
                            s_wqkvT[:, j * 768 + ocs: j * 768 + ocs + 128],
                            xln_fm[g % 2][:, j * 512:(j + 1) * 512],
                            start=(j == 0), stop=False))

            for c8 in range(8):
                tseq = g * 8 + c8
                if tseq == 0:
                    W("tensor", "ldc", ldc_all)
                W("tensor", "dve", trb_evac.get(tseq - 2, 0))
                if c8 == 0 and g > 1:
                    W("tensor", "dve", qk_evac[g - 2])
                    W("tensor", "dve", v_evac[(g - 2, 1)])
                p_tr = None
                for tix in range(4):
                    tile = 4 * g + tix
                    if c8 == 0:
                        W("tensor", "dve", app_c[tile])
                    p_tr = I("tensor", "pe", op(
                        "transpose",
                        tr_ps[tseq % 2][:, tix * 128:(tix + 1) * 128],
                        xln_tm[tile % 4][:, c8 * 128:(c8 + 1) * 128],
                        s_ident[:]))
                W("vector", "pe", p_tr)
                trb_evac[tseq] = I("vector", "dve", op(
                    "tensor_copy", xln_fm[g % 2][:, c8 * 512:(c8 + 1) * 512],
                    tr_ps[tseq % 2][:]))
                if c8 >= 1:
                    qkv_icc(c8 - 1)
            tr_grp[g] = cnt["pe"]
            qkv_icc(7)
            evac32 = trb_evac[g * 8 + 7]
            p_qk = None
            for p in range(2):
                for path in range(2):
                    bank = qk_ps[p * 2 + path]
                    ocs = path * 256 + p * 128
                    p_qk = I("tensor", "pe", op(
                        "matmul", bank[:], s_bqkv[:, ocs:ocs + 128],
                        s_ones_b[:], start=False, stop=True))
            W("vector", "pe", p_qk)
            sl = slice(g * 512, (g + 1) * 512)
            for p in range(2):
                E("vector", op("tensor_mul", q_sb[p][:, sl], qk_ps[p * 2][:],
                               f_tab[:, sl]))
                qk_evac[g] = I("vector", "dve", op(
                    "tensor_mul", k_sb[p][:, sl], qk_ps[p * 2 + 1][:],
                    f_tab[:, sl]))

            for half in range(2):
                if half == 0:
                    W("tensor", "dve", evac32)
                    if g > 0:
                        W("tensor", "dve", v_evac[(g - 1, 1)])
                else:
                    W("tensor", "dve", v_evac[(g, 0)])
                p_v = None
                for sub2 in range(2):
                    sub = half * 2 + sub2
                    vslc = v_ps[:, sub2 * 256:(sub2 + 1) * 256]
                    for icc in range(8):
                        E("tensor", op(
                            "matmul", vslc,
                            xln_fm[g % 2][:, icc * 512 + sub * 128:
                                          icc * 512 + (sub + 1) * 128],
                            s_wqkvT[:, icc * 768 + 512: icc * 768 + 768],
                            start=(icc == 0), stop=False))
                    p_v = I("tensor", "pe", op(
                        "matmul", vslc, s_ones_b[0:1, 0:128],
                        s_bqkv[:, 512:768], start=False, stop=True))
                W("vector", "pe", p_v)
                ev = None
                for sub2 in range(2):
                    kblk = g * 4 + half * 2 + sub2
                    for h in range(HPC):
                        ev = I("vector", "dve", op(
                            "tensor_copy",
                            v_sb[h][:, kblk * 65: kblk * 65 + 64],
                            v_ps[:, sub2 * 256 + h * 64:
                                 sub2 * 256 + (h + 1) * 64]))
                v_evac[(g, half)] = ev

            for p in range(2):
                for path in range(2):
                    src = (q_sb if path == 0 else k_sb)[p]
                    nm = path
                    W("scalar", "dve", qk_evac[g])
                    W("scalar", "pe", last_nsq_pe[nm])
                    a_n = I("scalar", "act", op(
                        "activation", s_nsq[nm][:], src[:, sl], AF.Square))
                    W("tensor", "act", a_n)
                    W("tensor", "dve", last_nrm_red)
                    p_n = I("tensor", "pe", op(
                        "matmul", nrm_ps[:], s_ones_c[:], s_nsq[nm][:],
                        start=True, stop=True))
                    last_nsq_pe[nm] = p_n
                    W("vector", "pe", p_n)
                    dst = (s_nq if path == 0 else s_nk)
                    last_nrm_red = I("vector", "dve", op(
                        "reduce_max", dst[0:1, 2 * g + p: 2 * g + p + 1],
                        nrm_ps[:], axis=AX.X))

        W("vector", "dve", last_nrm_red)
        E("vector", op("reduce_max", s_sm[0:1, 0:1], s_nq[:], axis=AX.X))
        sc1 = I("vector", "dve", op("reduce_max", s_sm[0:1, 1:2], s_nk[:],
                                    axis=AX.X))
        W("vector", "dve", sc1)
        sc2 = I("vector", "dve", op("tensor_mul", s_sm[0:1, 2:3],
                                    s_sm[0:1, 0:1], s_sm[0:1, 1:2]))
        W("scalar", "dve", sc2)
        sc3 = I("scalar", "act", op("activation", s_sm[0:1, 3:4],
                                    s_sm[0:1, 2:3], AF.Sqrt))
        W("vector", "act", sc3)
        sc4 = I("vector", "dve", op("tensor_scalar", s_sm[0:1, 4:5],
                                    s_sm[0:1, 3:4], -0.125, SHIFT_MARGIN,
                                    ALU.mult, ALU.add))
        W("vector", "dve", sc4)
        sc5 = I("vector", "dve", op("tensor_scalar_min", s_sm[0:1, 5:6],
                                    s_sm[0:1, 4:5], 0.0))
        W("tensor", "dve", sc5)
        W("tensor", "dve", qk_evac[NGRP - 1])
        p_sh = I("tensor", "pe", op("matmul", qk_ps[0][:, 0:1], s_ones_f[:],
                                    s_sm[0:1, 5:6], start=True, stop=True))
        W("vector", "pe", p_sh)
        I("vector", "dve", op("tensor_copy", s_shiftn[:], qk_ps[0][:, 0:1]))

        ps01.close()
        st0.close()

        ps2 = ExitStack()
        s_cell = [psum(ps2, f"s_cell{i}", [128, 1024]) for i in range(2)]
        y_ps = [psum(ps2, f"y_ps{i}", [65, 512]) for i in range(4)]

        st2 = ExitStack()
        e_cell = [sb(st2, f"e_cell{i}", [128, 1024], BF16) for i in range(3)]
        yev = sb(st2, "yev", [128, 512], F32)
        ynrm = [sb(st2, f"ynrm{i}", [128, 512], BF16) for i in range(2)]
        rcp = sb(st2, "rcp", [65, 512], F32)

        ph1_dve, ph1_pe, ph1_act = cnt["dve"], cnt["pe"], cnt["act"]
        for eng in ("tensor", "scalar", "vector", "gpsimd"):
            W(eng, "dve", ph1_dve)
            W(eng, "pe", ph1_pe)
            W(eng, "act", ph1_act)

        exp_c, attv_c, gate_c = {}, {}, {}
        ser = 0
        ytile = 0
        y_dma, y_mult = {}, {}
        pending_norm = [None]

        QG = NGRP // 4
        w_cast = {}
        for wi, wsrc in enumerate((w_proj, w_fr, w_fk, w_fv)):
            w_cast[wi] = I("gpsimd", "ldw2", op(
                "dma_start", out=w_bf[wi], in_=wsrc), 16)
        for qt in reversed(range(NGRP)):
            for pair in range(2):
                hA, hB = 2 * pair, 2 * pair + 1
                nkb = 4 * (qt + 1)
                qsl = slice(qt * 512, (qt + 1) * 512)
                base = ser

                def scores_emit(j):
                    sj = base + j
                    cell = s_cell[sj % 2]
                    ksl = slice(j * 128, (j + 1) * 128)
                    if sj >= 2:
                        W("tensor", "sE", exp_c[sj - 2])
                    E("tensor", op("matmul", cell[:, 0:512],
                                   k_sb[pair][0:64, ksl],
                                   q_sb[pair][0:64, qsl],
                                   start=True, stop=True))
                    sS_c = I("tensor", "sS", op(
                        "matmul", cell[:, 512:1024], k_sb[pair][64:128, ksl],
                        q_sb[pair][64:128, qsl], start=True, stop=True))
                    W("scalar", "sS", sS_c)
                    if sj >= 3:
                        W("scalar", "sV", attv_c[sj - 3])
                    exp_c[sj] = I("scalar", "sE", op(
                        "activation", e_cell[sj % 3][:], s_cell[sj % 2][:],
                        AF.Exp, scale=0.125, bias=s_shiftn[:]))
                    if j >= nkb - 4:
                        o384 = 384 - 128 * (j - (nkb - 4))
                        ecl = e_cell[sj % 3]
                        W("vector", "sE", exp_c[sj])
                        E("vector", op("tensor_mul", ecl[:, 0:512],
                                       ecl[:, 0:512],
                                       s_maskd[:, o384:o384 + 512]))
                        gate_c[sj] = ("sM", I("vector", "sM", op(
                            "tensor_mul", ecl[:, 512:1024], ecl[:, 512:1024],
                            s_maskd[:, o384:o384 + 512])))
                    else:
                        gate_c[sj] = ("sE", exp_c[sj])

                def attv_emit(j, yset):
                    sj = base + j
                    ecl = e_cell[sj % 3]
                    gs, gv = gate_c[sj]
                    if gs == "sM":
                        W("tensor", "sE", exp_c[sj])
                    W("tensor", gs, gv)
                    if j == 0 and ytile >= 2:
                        W("tensor", "dve", y_mult[ytile - 2])
                    E("tensor", op("matmul", y_ps[2 * yset][:],
                                   v_sb[hA][:, j * 65:(j + 1) * 65],
                                   ecl[:, 0:512], start=(j == 0),
                                   stop=(j == nkb - 1)))
                    attv_c[sj] = I("tensor", "sV", op(
                        "matmul", y_ps[2 * yset + 1][:],
                        v_sb[hB][:, j * 65:(j + 1) * 65],
                        ecl[:, 512:1024], start=(j == 0),
                        stop=(j == nkb - 1)))

                yset = ytile % 2
                scores_emit(0)
                if nkb > 1:
                    scores_emit(1)
                for j in range(nkb):
                    if j + 2 < nkb:
                        scores_emit(j + 2)
                    if j == min(4, nkb - 1) and pending_norm[0] is not None:
                        pending_norm[0]()
                        pending_norm[0] = None
                    attv_emit(j, yset)
                ser += nkb

                W("vector", "sV", attv_c[ser - 1])
                if ytile >= 1:
                    W("vector", "dve", y_mult[ytile - 1])
                yA, yB = y_ps[2 * yset], y_ps[2 * yset + 1]
                E("vector", op("reciprocal", rcp[0:1, :], yA[64:65, :]))
                E("vector", op("reciprocal", rcp[64:65, :], yB[64:65, :]))
                E("vector", op("tensor_copy", yev[0:64, :], yA[0:64, :]))
                d_ev = I("vector", "dve", op(
                    "tensor_copy", yev[64:128, :], yB[0:64, :]))

                def mk_norm(qt=qt, pair=pair, n=ytile, yset=yset, d_ev=d_ev):
                    def norm():
                        yA, yB = y_ps[2 * yset], y_ps[2 * yset + 1]
                        W("tensor", "dve", d_ev)
                        E("tensor", op("matmul", yA[0:64, :],
                                       s_ones_f[0:1, 0:64], rcp[0:1, :],
                                       start=True, stop=True))
                        p_bc = I("tensor", "pe", op(
                            "matmul", yB[0:64, :], s_ones64[64:65, :],
                            rcp[64:65, :], start=True, stop=True))
                        W("vector", "pe", p_bc)
                        if n >= 2:
                            W("vector", f"sty{n % 2}", y_dma[n - 2])
                        yt = ynrm[n % 2]
                        E("vector", op("tensor_mul", yt[0:64, :],
                                       yev[0:64, :], yA[0:64, :]))
                        y_mult[n] = I("vector", "dve", op(
                            "tensor_mul", yt[64:128, :], yev[64:128, :],
                            yB[0:64, :]))
                        W("sync", "dve", y_mult[n])
                        iq = qt // QG
                        y_dma[n] = I("sync", f"sty{n % 2}", op(
                            "dma_start",
                            out=cc_in_q[iq][pair * 128:(pair + 1) * 128,
                                            (qt % QG) * 512:
                                            (qt % QG + 1) * 512],
                            in_=yt[:]), 16)
                    return norm
                pending_norm[0] = mk_norm()
                ytile += 1
            if qt % QG == 0:
                if pending_norm[0] is not None:
                    pending_norm[0]()
                    pending_norm[0] = None
                iq = qt // QG
                W("gpsimd", "sty0", cnt["sty0"])
                W("gpsimd", "sty1", cnt["sty1"])
                I("gpsimd", "scc", op(
                    "collective_compute", "AllGather", ALU.bypass,
                    replica_groups=GROUPS, ins=[cc_in_q[iq]],
                    outs=[cc_out_all[iq * C:(iq + 1) * C, :]]))

        ph2_dve = cnt["dve"]

        stw = ExitStack()
        w_projT = sb(stw, "w_projT", [128, 8 * 1024], BF16, side="right")
        w_frT = sb(stw, "w_frT", [128, 8 * 1024], BF16, side="right")
        w_fkT = sb(stw, "w_fkT", [128, 8 * 1024], BF16, side="right")
        w_fvT = sb(stw, "w_fvT", [128, 8 * 1024], BF16, side="right")

        ps2.close()
        st2.close()
        st1.close()

        st4 = ExitStack()
        y_tile = [sb(st4, f"y_tile{i}", [128, 8 * 512], BF16)
                  for i in range(2)]
        y_halo = sb(st4, "y_halo", [128, 8 * 128], BF16)
        xr_t = [sb(st4, f"xr_t{i}", [128, C], F32) for i in range(2)]
        out1 = [sb(st4, f"out1{i}", [128, 4 * C], F32) for i in range(2)]
        h2_tm = [sb(st4, f"h2_tm{i}", [128, C], BF16) for i in range(2)]
        h2_fm = sb(st4, "h2_fm", [128, 8 * 513], BF16)
        hstage = sb(st4, "hstage", [128, 8], BF16)
        p4sq = sb(st4, "p4sq", [128, C], BF16)
        p4st = [sb(st4, f"p4st{i}", [128, 5], F32) for i in range(7)]
        d_mix = sb(st4, "d_mix", [128, 8 * 512], BF16)
        xr_fm = sb(st4, "xr_fm", [128, 8 * 512], BF16)
        xk_fm = sb(st4, "xk_fm", [128, 8 * 512], BF16)
        kact = sb(st4, "kact", [128, 8 * 512], BF16)
        ksc = sb(st4, "ksc", [128, 512], F32)
        r_f32 = sb(st4, "r_f32", [128, C], F32)
        fin = [sb(st4, f"fin{i}", [128, C], F32) for i in range(3)]

        ps4 = ExitStack()
        pr_ps = psum(ps4, "pr_ps", [128, 1024])
        t4_ps = [psum(ps4, f"t4_ps{i}", [128, 128], BF16) for i in range(2)]
        rr_ps = psum(ps4, "rr_ps", [128, 1024])
        kv_ps = psum(ps4, "kv_ps", [128, 1024])

        tr4_evac = {-2: 0, -1: 0}
        t4i_box = [0]
        W("scalar", "ldw2", cnt["ldw2"])
        for wi, wdst in enumerate((w_projT, w_frT, w_fkT, w_fvT)):
            for icc in range(8):
                I("scalar", "ldwt", op(
                    "dma_start", out=wdst[:, icc * 1024:(icc + 1) * 1024],
                    in_=w_bf[wi][:, icc * 128:(icc + 1) * 128],
                    transpose=True), 16)
        ldwt_all = cnt["ldwt"]

        def sy_pid(e):
            qv = e.partition_id() % 4
            sy_pid.qrow = e.snap(qv * 8, donate=True, min_val=0, max_val=32)
            sy_pid.hrow = e.snap(((qv + 3) % 4) * 8, donate=True,
                                 min_val=0, max_val=32)
        E("sync", sy_pid)
        cc_view = cc_out_all.rearrange("(r b) t -> b r t", b=128)

        resid_d, xres_ld = {}, {}
        p4sq_act = []
        tr4_done = {}
        last_apply_box = [0]

        def ln2_chain(cols, prev_apply):
            W("vector", "dve", prev_apply)
            E("vector", op("tensor_scalar_mul", p4st[2][:, cols],
                           p4st[0][:, cols], 1.0 / C))
            k1 = I("vector", "dve", op("tensor_scalar_mul", p4st[3][:, cols],
                                       p4st[1][:, cols], 1.0 / C))
            W("vector", "dve", k1)
            k2 = I("vector", "dve", op("scalar_tensor_tensor", p4st[4][:, cols],
                                       p4st[2][:, cols], -1.0, p4st[2][:, cols],
                                       ALU.mult, ALU.mult))
            W("vector", "dve", k2)
            k3 = I("vector", "dve", op("tensor_add", p4st[3][:, cols],
                                       p4st[3][:, cols], p4st[4][:, cols]))
            W("scalar", "dve", k3)
            k4 = I("scalar", "act", op("activation", p4st[4][:, cols],
                                       p4st[3][:, cols], AF.Sqrt,
                                       bias=s_eps[:]))
            W("vector", "act", k4)
            k5 = I("vector", "dve", op("reciprocal", p4st[5][:, cols],
                                       p4st[4][:, cols]))
            return k5

        def emit_proj(tile, lhs_fn, stcol, o1, pe_waits, act_waits=()):
            if tile >= 2:
                W("sync", "dve", resid_d[tile - 2])
            xres_ld[tile] = I("sync", f"ldr{tile % 2}", op(
                "dma_start", out=xr_t[tile % 2][:],
                in_=x_res[tile * 128:(tile + 1) * 128, :]), 16)
            for sm, vv in pe_waits:
                W("tensor", sm, vv)
            if tile >= 1:
                W("tensor", "dve", resid_d.get(tile - 1, 0))
            p_pr = None
            for half in range(2):
                o = pr_ps[:, half * 512:(half + 1) * 512]
                for c8 in range(8):
                    E("tensor", op("matmul", o, lhs_fn(c8),
                                   w_projT[:, c8 * 1024 + half * 512:
                                           c8 * 1024 + half * 512 + 512],
                                   start=(c8 == 0), stop=False))
                p_pr = I("tensor", "pe", op(
                    "matmul", o, s_ones_b[0:1, 0:128],
                    s_bproj[:, half * 512:half * 512 + 512],
                    start=False, stop=True))
            W("vector", "pe", p_pr)
            W("vector", f"ldr{tile % 2}", xres_ld[tile])
            for sm, vv in act_waits:
                W("vector", sm, vv)
            E("vector", op("tensor_add", o1[:, 0:512], pr_ps[:, 0:512],
                           xr_t[tile % 2][:, 0:512]))
            resid_d[tile] = I("vector", "dve", op(
                "tensor_add", o1[:, 512:1024], pr_ps[:, 512:1024],
                xr_t[tile % 2][:, 512:1024]))
            W("scalar", "dve", resid_d[tile])
            if p4sq_act:
                W("scalar", "act", p4sq_act[-1])
            a_sq = I("scalar", "act", op(
                "activation", p4sq[:], o1[:], AF.Square,
                accum_out=p4st[1][:, stcol:stcol + 1]))
            p4sq_act.append(a_sq)
            W("vector", "dve", resid_d[tile])
            d_rs = I("vector", "dve", op(
                "reduce_sum", p4st[0][:, stcol:stcol + 1], o1[:], axis=AX.X))
            return a_sq, d_rs

        def emit_apply_tr(tile, stcol, o1, halo):
            W("vector", "pe", tr4_done.get(tile - 2, 0))
            a_ap = I("vector", "dve", op(
                "tensor_scalar", h2_tm[tile % 2][:], o1[:],
                p4st[2][:, stcol:stcol + 1], p4st[5][:, stcol:stcol + 1],
                ALU.subtract, ALU.mult))
            last_apply_box[0] = a_ap
            for c8 in range(8):
                t4i = t4i_box[0]
                if c8 == 0:
                    W("tensor", "dve", a_ap)
                W("tensor", "dve", tr4_evac[t4i - 2])
                p_tr = I("tensor", "pe", op(
                    "transpose", t4_ps[t4i % 2][:],
                    h2_tm[tile % 2][:, c8 * 128:(c8 + 1) * 128], s_ident[:]))
                W("vector", "pe", p_tr)
                if halo:
                    tr4_evac[t4i] = I("vector", "dve", op(
                        "tensor_scalar_mul", h2_fm[:, c8 * 513:c8 * 513 + 1],
                        t4_ps[t4i % 2][:, 127:128], s_halo[:, 0:1]))
                else:
                    sub = (tile - 1) % 4
                    tr4_evac[t4i] = I("vector", "dve", op(
                        "tensor_copy",
                        h2_fm[:, c8 * 513 + 1 + sub * 128:
                              c8 * 513 + 1 + (sub + 1) * 128],
                        t4_ps[t4i % 2][:]))
                t4i_box[0] += 1
            tr4_done[tile] = cnt["pe"]

        W("sync", "scc", 4)
        yhl = I("sync", "ldyh", (
            lambda e: e.dma_start(
                out=y_halo[:].rearrange("p (a t) -> p a t", t=128),
                in_=cc_view[:, bass.ds(sy_pid.hrow, 8),
                            QSPAN - 128:QSPAN])), 8 * 16)
        out1h = out1[1][:, 0:C]
        a_sq, d_rs = emit_proj(
            0, lambda c8: y_halo[:, c8 * 128:(c8 + 1) * 128], 4, out1h,
            pe_waits=(("ldwt", ldwt_all), ("ldyh", yhl)))
        W("vector", "act", a_sq)
        W("vector", "dve", d_rs)
        ch = ln2_chain(slice(4, 5), 0)
        W("vector", "dve", ch)
        emit_apply_tr(0, 4, out1h, halo=True)
        halo_sq_act = a_sq
        halo_apply = last_apply_box[0]

        proj_pe_done = {}
        f2_prev = [0]
        mix_done_of = {}
        stripe_f2 = {}
        stripe_pe_end = 0
        kact_act = {}
        last_sig = {}
        fin_store = {}
        fin_m = {}

        def emit_ffn(sf):
            nonlocal stripe_pe_end, mix_done_of
            o1 = out1[sf % 2]
            for occ in range(8):
                par = occ % 2
                kslc = kv_ps[:, par * 512:(par + 1) * 512]
                if occ == 0:
                    W("tensor", "dve", mix_done_of[sf])
                if occ >= 2:
                    W("tensor", "act", kact_act.get((sf, occ - 2), 0))
                p_k = None
                for icc in range(8):
                    f = op("matmul", kslc,
                           w_fkT[:, icc * 1024 + occ * 128:
                                 icc * 1024 + (occ + 1) * 128],
                           xk_fm[:, icc * 512:(icc + 1) * 512],
                           start=(icc == 0), stop=(icc == 7))
                    if icc == 7:
                        p_k = I("tensor", "pe", f)
                    else:
                        E("tensor", f)
                W("vector", "pe", p_k)
                if occ >= 1:
                    W("vector", "act", kact_act[(sf, occ - 1)])
                elif sf > 0:
                    W("vector", "act", kact_act[(sf - 1, 7)])
                d_rl = I("vector", "dve", op("tensor_scalar_max", ksc[:],
                                             kslc, 0.0))
                W("scalar", "dve", d_rl)
                if sf > 0 and occ == 0:
                    W("scalar", "pe", stripe_pe_end)
                kact_act[(sf, occ)] = I("scalar", "act", op(
                    "activation", kact[:, occ * 512:(occ + 1) * 512], ksc[:],
                    AF.Square))

            for sub in range(4):
                idx = sf * 4 + sub
                if sub == 0:
                    W("tensor", "dve", mix_done_of[sf])
                else:
                    W("tensor", "act", last_sig[idx - 1])
                p_r = None
                for half in range(2):
                    o = rr_ps[:, half * 512:(half + 1) * 512]
                    for c8 in range(8):
                        f = op("matmul", o,
                               xr_fm[:, c8 * 512 + sub * 128:
                                     c8 * 512 + (sub + 1) * 128],
                               w_frT[:, c8 * 1024 + half * 512:
                                     c8 * 1024 + half * 512 + 512],
                               start=(c8 == 0), stop=(c8 == 7))
                        if half == 1 and c8 == 7:
                            p_r = I("tensor", "pe", f)
                        else:
                            E("tensor", f)
                W("scalar", "pe", p_r)
                if idx >= 1:
                    W("scalar", "dve", fin_m[idx - 1])
                last_sig[idx] = I("scalar", "act", op(
                    "activation", r_f32[:], rr_ps[:], AF.Sigmoid))
                if sub == 0:
                    W("tensor", "act", kact_act[(sf, 7)])
                else:
                    W("tensor", "dve", fin_m[idx - 1])
                p_v = None
                for half in range(2):
                    o = kv_ps[:, half * 512:(half + 1) * 512]
                    for c8 in range(8):
                        f = op("matmul", o,
                               kact[:, c8 * 512 + sub * 128:
                                    c8 * 512 + (sub + 1) * 128],
                               w_fvT[:, c8 * 1024 + half * 512:
                                     c8 * 1024 + half * 512 + 512],
                               start=(c8 == 0), stop=(c8 == 7))
                        if half == 1 and c8 == 7:
                            p_v = I("tensor", "pe", f)
                        else:
                            E("tensor", f)
                W("vector", "pe", p_v)
                W("vector", "act", last_sig[idx])
                if f2_prev[0]:
                    W("vector", "dve", f2_prev[0])
                E("vector", op("tensor_mul", fin[0][:, 0:512],
                               kv_ps[:, 0:512], r_f32[:, 0:512]))
                fin_m[idx] = I("vector", "dve", op(
                    "tensor_mul", fin[0][:, 512:1024], kv_ps[:, 512:1024],
                    r_f32[:, 512:1024]))
                W("vector", "dve", fin_m[idx])
                if idx >= 2:
                    W("vector", f"sto{idx % 2}", fin_store[idx - 2])
                fb = fin[1 + idx % 2]
                E("vector", op("tensor_add", fb[:, 0:512], fin[0][:, 0:512],
                               o1[:, sub * 1024:sub * 1024 + 512]))
                f2 = I("vector", "dve", op(
                    "tensor_add", fb[:, 512:1024], fin[0][:, 512:1024],
                    o1[:, sub * 1024 + 512:(sub + 1) * 1024]))
                f2_prev[0] = f2
                W("sync", "dve", f2)
                fin_store[idx] = I("sync", f"sto{idx % 2}", op(
                    "dma_start",
                    out=out_t[idx * 128:(idx + 1) * 128, :], in_=fb[:]), 16)
            stripe_pe_end = cnt["pe"]
            stripe_f2[sf] = f2_prev[0]

        for s in range(NSTR):
            if s >= 2:
                W("sync", "pe", proj_pe_done[s - 2])
            ylast = I("sync", "ldy", (
                lambda e, s=s: e.dma_start(
                    out=y_tile[s % 2][:].rearrange("p (a t) -> p a t", t=512),
                    in_=cc_view[:, bass.ds(sy_pid.qrow, 8),
                                s * 512:(s + 1) * 512])), 8 * 16)

            st_list = []
            for sub in range(4):
                tile = 1 + 4 * s + sub
                pw = [("ldy", ylast)] if sub == 0 else []
                aw = []
                if s >= 2 and sub == 0:
                    aw = [("dve", stripe_f2[s - 2])]
                a_sq, d_rs = emit_proj(
                    tile,
                    (lambda c8, sub=sub, s=s:
                     y_tile[s % 2][:, c8 * 512 + sub * 128:
                                   c8 * 512 + (sub + 1) * 128]),
                    sub, out1[s % 2][:, sub * 1024:(sub + 1) * 1024],
                    pe_waits=pw, act_waits=aw)
                st_list.append((a_sq, d_rs))
            proj_pe_done[s] = cnt["pe"]
            W("vector", "act", st_list[-1][0])
            W("vector", "dve", st_list[-1][1])
            ch = ln2_chain(slice(0, 4), last_apply_box[0])
            W("vector", "dve", ch)

            if s > 0:
                emit_ffn(s - 1)
                W("vector", "dve", mix_done_of[s - 1])
                hv = h2_fm[:].rearrange("p (c w) -> p c w", w=513)
                b0 = I("vector", "dve", op("tensor_copy", hstage[:],
                                           hv[:, :, 512]))
                W("vector", "dve", b0)
                b1 = I("vector", "dve", op("tensor_copy", hv[:, :, 0],
                                           hstage[:]))
                W("vector", "dve", b1)

            for sub in range(4):
                tile = 1 + 4 * s + sub
                emit_apply_tr(tile, sub,
                              out1[s % 2][:, sub * 1024:(sub + 1) * 1024],
                              halo=False)
            evac_done = cnt["dve"]

            W("vector", "dve", evac_done)
            W("vector", "ldmu", ldmu_all)
            if s > 0:
                W("vector", "pe", stripe_pe_end)
            dm = None
            for c8 in range(8):
                cur = h2_fm[:, c8 * 513 + 1: c8 * 513 + 513]
                prv = h2_fm[:, c8 * 513: c8 * 513 + 512]
                dm = I("vector", "dve", op(
                    "tensor_sub", d_mix[:, c8 * 512:(c8 + 1) * 512], cur, prv))
            W("vector", "dve", dm)
            mx = None
            for c8 in range(8):
                prv = h2_fm[:, c8 * 513: c8 * 513 + 512]
                E("vector", op("scalar_tensor_tensor",
                               xr_fm[:, c8 * 512:(c8 + 1) * 512],
                               d_mix[:, c8 * 512:(c8 + 1) * 512],
                               s_mur[:, c8:c8 + 1], prv, ALU.mult, ALU.add))
                mx = I("vector", "dve", op(
                    "scalar_tensor_tensor", xk_fm[:, c8 * 512:(c8 + 1) * 512],
                    d_mix[:, c8 * 512:(c8 + 1) * 512],
                    s_muk[:, c8:c8 + 1], prv, ALU.mult, ALU.add))
            mix_done_of[s] = mx
        emit_ffn(NSTR - 1)

        W("sync", "sto0", cnt["sto0"])
        W("sync", "sto1", cnt["sto1"])
        W("gpsimd", "sto0", cnt["sto0"])
        W("gpsimd", "sto1", cnt["sto1"])

        for eng in ENGS:
            def body(e, ops=tuple(q[eng]), eng=eng):
                for fi, f in enumerate(ops):
                    try:
                        f(e)
                    except Exception as ex:
                        import traceback
                        traceback.print_exc()
                        raise RuntimeError(
                            f"flush fail {eng}[{fi}]: "
                            f"{getattr(f, 'opname', None) or [getattr(d, 'opname', None) for d in (f.__defaults__ or [])]}"
                        ) from ex
            getattr(block, eng)(body)

        ps4.close()
        st4.close()
        stw.close()

    return nc



def _mask_host():
    p = np.arange(128)[:, None]
    g = np.arange(896)[None, :]
    return (p <= g - 384).astype(ml_dtypes.bfloat16)


def prep_inputs(inp, T):
    x = np.ascontiguousarray(np.asarray(inp["L0_attn_x"], dtype=np.float32))
    W_attn = np.asarray(inp["L0_attn_W_attn"], dtype=np.float32)
    b_attn = np.asarray(inp["L0_attn_b_attn"], dtype=np.float32)
    TSL = T // 4
    shared = dict(
        w_proj=np.ascontiguousarray(np.asarray(inp["L0_attn_W_proj"], np.float32)),
        b_proj=np.asarray(inp["L0_attn_b_proj"], np.float32).reshape(1, C),
        w_fr=np.ascontiguousarray(np.asarray(inp["L0_ffn_W_r"], np.float32)),
        w_fk=np.ascontiguousarray(np.asarray(inp["L0_ffn_W_k"], np.float32)),
        w_fv=np.ascontiguousarray(np.asarray(inp["L0_ffn_W_v"], np.float32)),
        mu_r=np.asarray(inp["L0_ffn_mu_r"], np.float32),
        mu_k=np.asarray(inp["L0_ffn_mu_k"], np.float32),
        identb=np.eye(128, dtype=ml_dtypes.bfloat16),
        maskd=_mask_host(),
    )
    in_maps = []
    for c in range(8):
        seq, qq = c // 4, c % 4
        heads = [4 * qq + h for h in range(HPC)]
        rows, brows = [], []
        for sec in range(3):
            for h in heads:
                rows.append(W_attn[sec * C + h * HD: sec * C + (h + 1) * HD])
                brows.append(b_attn[sec * C + h * HD: sec * C + (h + 1) * HD])
        xs = x[seq, :T]
        t0 = qq * TSL
        halo_rows = (np.zeros((128, C), np.float32) if qq == 0
                     else xs[t0 - 128: t0])
        in_maps.append(dict(
            shared,
            x_seq=np.ascontiguousarray(xs),
            x_res=np.ascontiguousarray(
                np.concatenate([halo_rows, xs[t0: t0 + TSL]], axis=0)),
            w_qkv=np.ascontiguousarray(np.concatenate(rows, axis=0)),
            b_qkv=np.concatenate(brows).reshape(1, 768),
            halo_s=np.full((128, 1), 0.0 if qq == 0 else 1.0, np.float32),
        ))
    return in_maps


_NC_CACHE = {}


def kernel(**inputs):
    T = 8192
    if T not in _NC_CACHE:
        _NC_CACHE[T] = build(T)
    nc = _NC_CACHE[T]
    in_maps = prep_inputs(inputs, T)
    res = run_bass_kernel_spmd(nc, in_maps, core_ids=list(range(8)))
    TSL = T // 4
    outp = np.empty((2, T, C), dtype=np.float32)
    for c in range(8):
        seq, qq = c // 4, c % 4
        outp[seq, qq * TSL:(qq + 1) * TSL] = res.results[c]["out"]
    return outp


# version 08
# speedup vs baseline: 1.2600x; 1.0044x over previous
import math
from contextlib import ExitStack

import numpy as np
import ml_dtypes

import concourse.bass as bass
import concourse.mybir as mybir
from concourse.bass_utils import run_bass_kernel_spmd

F32 = mybir.dt.float32
BF16 = mybir.dt.bfloat16
I32 = mybir.dt.int32
AF = mybir.ActivationFunctionType
ALU = mybir.AluOpType
AX = mybir.AxisListType

C = 1024
HD = 64
HPC = 4
GROUPS = [[0, 1, 2, 3], [4, 5, 6, 7]]
LN_EPS = 1e-5
SHIFT_MARGIN = 60.0
RT2 = math.sqrt(2.0)
TWO_PI = 2.0 * math.pi
ENGS = ("sync", "gpsimd", "tensor", "scalar", "vector")


def op(name, *args, **kw):
    f = lambda e: getattr(e, name)(*args, **kw)
    f.opname = name
    f.opargs = args
    return f


def build(T=8192):
    assert T % 2048 == 0
    TSL = T // 4
    NGRP = T // 512
    NTILP = TSL // 128
    NSTR = TSL // 512

    nc = bass.Bass("TRN2", target_bir_lowering=False)

    x_seq = nc.dram_tensor("x_seq", [T, C], F32, kind="ExternalInput").ap()
    x_res = nc.dram_tensor("x_res", [TSL + 128, C], F32, kind="ExternalInput").ap()
    w_qkv = nc.dram_tensor("w_qkv", [768, C], F32, kind="ExternalInput").ap()
    b_qkv = nc.dram_tensor("b_qkv", [1, 768], F32, kind="ExternalInput").ap()
    w_proj = nc.dram_tensor("w_proj", [C, C], F32, kind="ExternalInput").ap()
    b_proj = nc.dram_tensor("b_proj", [1, C], F32, kind="ExternalInput").ap()
    w_fr = nc.dram_tensor("w_fr", [C, C], F32, kind="ExternalInput").ap()
    w_fk = nc.dram_tensor("w_fk", [C, C], F32, kind="ExternalInput").ap()
    w_fv = nc.dram_tensor("w_fv", [C, C], F32, kind="ExternalInput").ap()
    mu_r = nc.dram_tensor("mu_r", [C], F32, kind="ExternalInput").ap()
    mu_k = nc.dram_tensor("mu_k", [C], F32, kind="ExternalInput").ap()
    identb = nc.dram_tensor("identb", [128, 128], BF16, kind="ExternalInput").ap()
    maskd = nc.dram_tensor("maskd", [128, 896], BF16, kind="ExternalInput").ap()
    halo_s = nc.dram_tensor("halo_s", [128, 1], F32, kind="ExternalInput").ap()
    out_t = nc.dram_tensor("out", [TSL, C], F32, kind="ExternalOutput").ap()

    QSPAN = T // 4
    cc_in_q = [nc.dram_tensor(f"cc_in{i}", [HPC * HD, QSPAN], BF16,
                              kind="Internal").ap() for i in range(4)]
    cc_out_all = nc.dram_tensor("cc_out_all", [4 * C, QSPAN], BF16,
                                kind="Internal").ap()
    w_bf = [nc.dram_tensor(f"w_bf{i}", [C, C], BF16, kind="Internal").ap()
            for i in range(4)]
    wq_bf = nc.dram_tensor("wq_bf", [768, C], BF16, kind="Internal").ap()

    q = {e: [] for e in ENGS}
    sems = {}
    cnt = {}

    def E(eng, f):
        q[eng].append(f)

    def W(eng, sem, val):
        if val and val > 0:
            q[eng].append(lambda e, s=sems[sem], v=val: e.wait_ge(s, v))

    def I(eng, sem, f, k=1):
        cnt[sem] += k
        q[eng].append(lambda e, f=f, s=sems[sem], kk=k: f(e).then_inc(s, kk))
        return cnt[sem]

    ctx = ExitStack()
    with ctx:
        def sb(st, name, shape, dt, side="left"):
            return st.enter_context(nc.sbuf_tensor(name, shape, dt, side=side))

        def psum(st, name, shape, dt=F32):
            return st.enter_context(nc.psum_tensor(name, shape, dt))

        for s in ("ldc", "ldx0", "ldx1", "ldx2", "ldx3", "ldx4", "ldx5",
                  "ldx6", "ldx7", "ldw", "ldwq",
                  "ldmu", "ldw2", "gp", "pe", "act", "dve", "sS", "sE", "sM",
                  "sV",
                  "sty0", "sty1", "scc", "ldyh", "ldy", "ldwt", "ldr0",
                  "ldr1",
                  "sto0", "sto1"):
            sems[s] = ctx.enter_context(nc.semaphore(s))
            cnt[s] = 0

        s_ident = sb(ctx, "s_ident", [128, 128], BF16)
        s_maskd = sb(ctx, "s_maskd", [128, 896], BF16)
        s_ones_b = sb(ctx, "s_ones_b", [1, 512], BF16)
        s_ones_c = sb(ctx, "s_ones_c", [128, 1], BF16)
        s_ones64 = sb(ctx, "s_ones64", [65, 64], F32)
        s_ones_f = sb(ctx, "s_ones_f", [1, 128], F32)
        s_bqkv = sb(ctx, "s_bqkv", [1, 768], BF16)
        s_bproj = sb(ctx, "s_bproj", [1, C], BF16)
        s_mur = sb(ctx, "s_mur", [128, 8], F32)
        s_muk = sb(ctx, "s_muk", [128, 8], F32)
        s_halo = sb(ctx, "s_halo", [128, 1], F32)
        s_shiftn = sb(ctx, "s_shiftn", [128, 1], F32)
        s_eps = sb(ctx, "s_eps", [128, 1], F32)
        s_b8 = sb(ctx, "s_b8", [64, 1], F32)
        s_nq = sb(ctx, "s_nq", [1, 2 * NGRP], F32)
        s_nk = sb(ctx, "s_nk", [1, 2 * NGRP], F32)
        s_sm = sb(ctx, "s_sm", [1, 8], F32)

        block = ctx.enter_context(nc.Block())

        st1 = ExitStack()
        q_sb = [sb(st1, f"q_sb{p}", [128, T], BF16) for p in range(2)]
        k_sb = [sb(st1, f"k_sb{p}", [128, T], BF16) for p in range(2)]
        v_sb = [sb(st1, f"v_sb{h}", [128, (T // 128) * 65], BF16)
                for h in range(HPC)]

        st0 = ExitStack()
        s_wqkvT = sb(st0, "s_wqkvT", [128, 8 * 768], BF16)
        f_tab = sb(st0, "f_tab", [128, T], BF16, side="right")
        s_theta = sb(st0, "s_theta", [1, 64], F32)
        s_seq = sb(st0, "s_seq", [1, 256], F32)
        s_u = sb(st0, "s_u", [64, 256], F32)
        s_kf = sb(st0, "s_kf", [64, 256], F32)
        s_d0 = sb(st0, "s_d0", [64, 256], F32)
        x_ring = [sb(st0, f"x_ring{i}", [128, C], F32) for i in range(8)]
        sq_scr = [sb(st0, f"sq_scr{i}", [128, C], BF16) for i in range(2)]
        xln_tm = [sb(st0, f"xln_tm{i}", [128, C], BF16) for i in range(4)]
        xln_fm = [sb(st0, f"xln_fm{i}", [128, 8 * 512], BF16)
                  for i in range(2)]
        stats_sq = [sb(st0, f"stats_sq{i}", [128, 4], F32) for i in range(2)]
        stats_su = [sb(st0, f"stats_su{i}", [128, 4], F32) for i in range(2)]
        stats_m = [sb(st0, f"stats_m{i}", [128, 4], F32) for i in range(2)]
        stats_v = sb(st0, "stats_v", [128, 4], F32)
        stats_t = [sb(st0, f"stats_t{i}", [128, 4], F32) for i in range(2)]
        stats_r = [sb(st0, f"stats_r{i}", [128, 4], F32) for i in range(2)]
        s_nsq = [sb(st0, f"s_nsq{i}", [128, 512], BF16) for i in range(2)]

        ps01 = ExitStack()
        tr_ps = [psum(ps01, f"tr_ps{i}", [128, 512], BF16) for i in range(2)]
        qk_ps = [psum(ps01, f"qk_ps{i}", [128, 512]) for i in range(4)]
        v_ps = psum(ps01, "v_ps", [128, 512])
        nrm_ps = psum(ps01, "nrm_ps", [1, 512])

        for tns, val in ((s_ones_b, 1.0), (s_ones_c, 1.0), (s_ones_f, 1.0),
                         (s_ones64, 1.0),
                         (s_eps, LN_EPS), (s_b8, 0.125), (s_sm, 0.0)):
            E("gpsimd", op("memset", tns[:], val))
        for h in range(HPC):
            E("gpsimd", op("memset", v_sb[h][:], 1.0))
        gp0 = I("gpsimd", "gp", op("iota", s_theta[:, 0:32], [[1, 32]],
                                   base=0, channel_multiplier=0,
                                   allow_small_or_imprecise_dtypes=True))
        I("gpsimd", "ldw", op("dma_start", out=s_bqkv[:], in_=b_qkv), 16)
        I("gpsimd", "ldw", op("dma_start", out=s_bproj[:], in_=b_proj), 16)
        ldw_wq = I("gpsimd", "ldw", op(
            "dma_start", out=wq_bf, in_=w_qkv), 16)
        for c8 in range(8):
            I("gpsimd", "ldmu", op("dma_start", out=s_mur[:, c8:c8 + 1],
                                   in_=mu_r[c8 * 128:(c8 + 1) * 128],
                                   allow_slow_non_contiguous=True), 16)
            I("gpsimd", "ldmu", op("dma_start", out=s_muk[:, c8:c8 + 1],
                                   in_=mu_k[c8 * 128:(c8 + 1) * 128],
                                   allow_slow_non_contiguous=True), 16)
        ldmu_all = cnt["ldmu"]

        I("sync", "ldc", op("dma_start", out=s_ident[:], in_=identb), 16)
        I("sync", "ldc", op("dma_start", out=s_maskd[:], in_=maskd), 16)
        ldc_all = I("sync", "ldc", op("dma_start", out=s_halo[:], in_=halo_s), 16)

        W("scalar", "gp", gp0)
        a_th = I("scalar", "act", op("activation", s_theta[:, 0:32],
                                     s_theta[:, 0:32], AF.Exp,
                                     scale=-math.log(10000.0) / 32.0))
        W("vector", "act", a_th)
        d_th = I("vector", "dve", op("tensor_copy", s_theta[:, 32:64],
                                     s_theta[:, 0:32]))

        tr_evac = {-2: 0, -1: 0}
        W("scalar", "ldw", cnt["ldw"])
        for icc in range(8):
            I("scalar", "ldwq", op(
                "dma_start", out=s_wqkvT[:, icc * 768:(icc + 1) * 768],
                in_=wq_bf[:, icc * 128:(icc + 1) * 128], transpose=True), 16)
        ldwq_all = cnt["ldwq"]

        a_u_prev = 0
        p_ang_prev = 0
        for g5 in range(T // 256):
            sl = slice(g5 * 256, (g5 + 1) * 256)
            if g5 == 0:
                W("tensor", "dve", d_th)
                W("tensor", "gp", gp0)
            else:
                W("tensor", "dve", a_u_prev)
            W("gpsimd", "pe", p_ang_prev)
            sq_i = I("gpsimd", "gp", op("iota", s_seq[:], [[1, 256]],
                                        base=g5 * 256 + 1,
                                        channel_multiplier=0,
                                        allow_small_or_imprecise_dtypes=True))
            W("tensor", "gp", sq_i)
            p_ang = I("tensor", "pe", op("matmul", qk_ps[0][0:64, 0:256],
                                         s_theta[:], s_seq[:],
                                         start=True, stop=True))
            p_ang_prev = p_ang
            W("vector", "pe", p_ang)
            a_u = I("vector", "dve", op("tensor_scalar", s_u[:],
                                        qk_ps[0][0:64, 0:256],
                                        1.0 / TWO_PI, 0.125,
                                        ALU.mult, ALU.add))
            a_u_prev = a_u
            W("vector", "dve", a_u)
            d1 = I("vector", "dve", op("tensor_scalar", s_kf[:], s_u[:],
                                       8388608.0, 8388608.0,
                                       ALU.add, ALU.subtract))
            W("vector", "dve", d1)
            d2 = I("vector", "dve", op("tensor_sub", s_d0[:], s_u[:], s_kf[:]))
            W("scalar", "dve", d2)
            a_s = I("scalar", "act", op("activation", s_u[:], s_d0[:], AF.Sin,
                                        scale=TWO_PI))
            W("vector", "act", a_s)
            d3 = I("vector", "dve", op("tensor_scalar_mul", f_tab[0:64, sl],
                                       s_u[:], RT2))
            W("vector", "dve", d3)
            I("vector", "dve", op("tensor_copy", f_tab[64:128, sl],
                                  f_tab[0:64, sl]))

        x_load, app_c, trb_evac, tr_grp = {}, {}, {}, {}
        sq_act = {}
        grp_chain5 = {}
        grp_c4 = {}
        qk_evac, v_evac = {}, {}
        grp_chain1 = {}
        last_nsq_pe = {0: 0, 1: 0}
        last_nrm_red = 0

        def ln_prep(g):
            gm = g % 2
            sqa = {}
            for tix in range(4):
                tile = 4 * g + tix
                if tile >= 8:
                    W("sync", "dve", app_c[tile - 8])
                x_load[tile] = I("sync", f"ldx{tile % 8}", op(
                    "dma_start", out=x_ring[tile % 8][:],
                    in_=x_seq[tile * 128:(tile + 1) * 128, :]), 16)
                W("scalar", f"ldx{tile % 8}", x_load[tile])
                if g >= 2 and tix == 0:
                    W("scalar", "dve", grp_chain1[g - 2])
                if tile >= 1:
                    W("scalar", "act", sq_act[tile - 1])
                I("scalar", "act", op(
                    "activation", sq_scr[tile % 2][:], x_ring[tile % 8][:],
                    AF.Square, accum_out=stats_sq[gm][:, tix:tix + 1]))
                sqa[tix] = I("scalar", "act", op(
                    "activation", sq_scr[(tile + 1) % 2][:],
                    x_ring[tile % 8][:],
                    AF.Copy, accum_out=stats_su[gm][:, tix:tix + 1]))
                sq_act[tile] = sqa[tix]

            W("vector", "act", sqa[3])
            if g >= 2:
                W("vector", "dve", grp_chain5[g - 2])
            E("vector", op("tensor_scalar_mul", stats_m[gm][:],
                           stats_su[gm][:], 1.0 / C))
            c1 = I("vector", "dve", op("tensor_scalar_mul", stats_v[:],
                                       stats_sq[gm][:], 1.0 / C))
            grp_chain1[g] = c1
            W("vector", "dve", c1)
            c2 = I("vector", "dve", op("scalar_tensor_tensor", stats_t[gm][:],
                                       stats_m[gm][:], -1.0, stats_m[gm][:],
                                       ALU.mult, ALU.mult))
            W("vector", "dve", c2)
            c3 = I("vector", "dve", op("tensor_add", stats_v[:], stats_v[:],
                                       stats_t[gm][:]))
            W("scalar", "dve", c3)
            c4 = I("scalar", "act", op("activation", stats_t[gm][:],
                                       stats_v[:], AF.Sqrt, bias=s_eps[:]))
            grp_c4[g] = c4
            return c4

        ln_prep(0)
        for g in range(NGRP):
            gm = g % 2
            W("vector", "act", grp_c4[g])
            c5 = I("vector", "dve", op("reciprocal", stats_r[gm][:],
                                       stats_t[gm][:]))
            grp_chain5[g] = c5
            if g + 1 < NGRP:
                ln_prep(g + 1)
            for tix in range(4):
                tile = 4 * g + tix
                W("vector", "dve", c5)
                if g >= 1 and tix == 0:
                    W("vector", "pe", tr_grp[g - 1])
                app_c[tile] = I("vector", "dve", op(
                    "tensor_scalar", xln_tm[tile % 4][:], x_ring[tile % 8][:],
                    stats_m[gm][:, tix:tix + 1], stats_r[gm][:, tix:tix + 1],
                    ALU.subtract, ALU.mult))

            def qkv_icc(j):
                W("tensor", "dve", trb_evac[g * 8 + j])
                if j == 0:
                    if g == 0:
                        W("tensor", "ldwq", ldwq_all)
                    else:
                        W("tensor", "dve", qk_evac[g - 1])
                for p in range(2):
                    for path in range(2):
                        bank = qk_ps[p * 2 + path]
                        ocs = path * 256 + p * 128
                        E("tensor", op(
                            "matmul", bank[:],
                            s_wqkvT[:, j * 768 + ocs: j * 768 + ocs + 128],
                            xln_fm[g % 2][:, j * 512:(j + 1) * 512],
                            start=(j == 0), stop=False))

            for c8 in range(8):
                tseq = g * 8 + c8
                if tseq == 0:
                    W("tensor", "ldc", ldc_all)
                W("tensor", "dve", trb_evac.get(tseq - 2, 0))
                if c8 == 0 and g > 1:
                    W("tensor", "dve", qk_evac[g - 2])
                    W("tensor", "dve", v_evac[(g - 2, 1)])
                p_tr = None
                for tix in range(4):
                    tile = 4 * g + tix
                    if c8 == 0:
                        W("tensor", "dve", app_c[tile])
                    p_tr = I("tensor", "pe", op(
                        "transpose",
                        tr_ps[tseq % 2][:, tix * 128:(tix + 1) * 128],
                        xln_tm[tile % 4][:, c8 * 128:(c8 + 1) * 128],
                        s_ident[:]))
                W("vector", "pe", p_tr)
                trb_evac[tseq] = I("vector", "dve", op(
                    "tensor_copy", xln_fm[g % 2][:, c8 * 512:(c8 + 1) * 512],
                    tr_ps[tseq % 2][:]))
                if c8 >= 1:
                    qkv_icc(c8 - 1)
            tr_grp[g] = cnt["pe"]
            qkv_icc(7)
            evac32 = trb_evac[g * 8 + 7]
            p_qk = None
            for p in range(2):
                for path in range(2):
                    bank = qk_ps[p * 2 + path]
                    ocs = path * 256 + p * 128
                    p_qk = I("tensor", "pe", op(
                        "matmul", bank[:], s_bqkv[:, ocs:ocs + 128],
                        s_ones_b[:], start=False, stop=True))
            W("vector", "pe", p_qk)
            sl = slice(g * 512, (g + 1) * 512)
            for p in range(2):
                E("vector", op("tensor_mul", q_sb[p][:, sl], qk_ps[p * 2][:],
                               f_tab[:, sl]))
                qk_evac[g] = I("vector", "dve", op(
                    "tensor_mul", k_sb[p][:, sl], qk_ps[p * 2 + 1][:],
                    f_tab[:, sl]))

            for half in range(2):
                if half == 0:
                    W("tensor", "dve", evac32)
                    if g > 0:
                        W("tensor", "dve", v_evac[(g - 1, 1)])
                else:
                    W("tensor", "dve", v_evac[(g, 0)])
                p_v = None
                for sub2 in range(2):
                    sub = half * 2 + sub2
                    vslc = v_ps[:, sub2 * 256:(sub2 + 1) * 256]
                    for icc in range(8):
                        E("tensor", op(
                            "matmul", vslc,
                            xln_fm[g % 2][:, icc * 512 + sub * 128:
                                          icc * 512 + (sub + 1) * 128],
                            s_wqkvT[:, icc * 768 + 512: icc * 768 + 768],
                            start=(icc == 0), stop=False))
                    p_v = I("tensor", "pe", op(
                        "matmul", vslc, s_ones_b[0:1, 0:128],
                        s_bqkv[:, 512:768], start=False, stop=True))
                W("vector", "pe", p_v)
                ev = None
                for sub2 in range(2):
                    kblk = g * 4 + half * 2 + sub2
                    for h in range(HPC):
                        ev = I("vector", "dve", op(
                            "tensor_copy",
                            v_sb[h][:, kblk * 65: kblk * 65 + 64],
                            v_ps[:, sub2 * 256 + h * 64:
                                 sub2 * 256 + (h + 1) * 64]))
                v_evac[(g, half)] = ev

            for p in range(2):
                for path in range(2):
                    src = (q_sb if path == 0 else k_sb)[p]
                    nm = path
                    W("scalar", "dve", qk_evac[g])
                    W("scalar", "pe", last_nsq_pe[nm])
                    a_n = I("scalar", "act", op(
                        "activation", s_nsq[nm][:], src[:, sl], AF.Square))
                    W("tensor", "act", a_n)
                    W("tensor", "dve", last_nrm_red)
                    p_n = I("tensor", "pe", op(
                        "matmul", nrm_ps[:], s_ones_c[:], s_nsq[nm][:],
                        start=True, stop=True))
                    last_nsq_pe[nm] = p_n
                    W("vector", "pe", p_n)
                    dst = (s_nq if path == 0 else s_nk)
                    last_nrm_red = I("vector", "dve", op(
                        "reduce_max", dst[0:1, 2 * g + p: 2 * g + p + 1],
                        nrm_ps[:], axis=AX.X))

        W("vector", "dve", last_nrm_red)
        E("vector", op("reduce_max", s_sm[0:1, 0:1], s_nq[:], axis=AX.X))
        sc1 = I("vector", "dve", op("reduce_max", s_sm[0:1, 1:2], s_nk[:],
                                    axis=AX.X))
        W("vector", "dve", sc1)
        sc2 = I("vector", "dve", op("tensor_mul", s_sm[0:1, 2:3],
                                    s_sm[0:1, 0:1], s_sm[0:1, 1:2]))
        W("scalar", "dve", sc2)
        sc3 = I("scalar", "act", op("activation", s_sm[0:1, 3:4],
                                    s_sm[0:1, 2:3], AF.Sqrt))
        W("vector", "act", sc3)
        sc4 = I("vector", "dve", op("tensor_scalar", s_sm[0:1, 4:5],
                                    s_sm[0:1, 3:4], -0.125, SHIFT_MARGIN,
                                    ALU.mult, ALU.add))
        W("vector", "dve", sc4)
        sc5 = I("vector", "dve", op("tensor_scalar_min", s_sm[0:1, 5:6],
                                    s_sm[0:1, 4:5], 0.0))
        W("tensor", "dve", sc5)
        W("tensor", "dve", qk_evac[NGRP - 1])
        p_sh = I("tensor", "pe", op("matmul", qk_ps[0][:, 0:1], s_ones_f[:],
                                    s_sm[0:1, 5:6], start=True, stop=True))
        W("vector", "pe", p_sh)
        I("vector", "dve", op("tensor_copy", s_shiftn[:], qk_ps[0][:, 0:1]))

        ps01.close()
        st0.close()

        ps2 = ExitStack()
        s_cell = [psum(ps2, f"s_cell{i}", [128, 1024]) for i in range(2)]
        y_ps = [psum(ps2, f"y_ps{i}", [65, 512]) for i in range(4)]

        st2 = ExitStack()
        e_cell = [sb(st2, f"e_cell{i}", [128, 1024], BF16) for i in range(3)]
        yev = sb(st2, "yev", [128, 512], F32)
        ynrm = [sb(st2, f"ynrm{i}", [128, 512], BF16) for i in range(2)]
        rcp = sb(st2, "rcp", [65, 512], F32)

        ph1_dve, ph1_pe, ph1_act = cnt["dve"], cnt["pe"], cnt["act"]
        for eng in ("tensor", "scalar", "vector", "gpsimd"):
            W(eng, "dve", ph1_dve)
            W(eng, "pe", ph1_pe)
            W(eng, "act", ph1_act)

        exp_c, attv_c, gate_c = {}, {}, {}
        ser = 0
        ytile = 0
        y_dma, y_mult = {}, {}
        pending_norm = [None]

        QG = NGRP // 4
        w_cast = {}
        for wi, wsrc in enumerate((w_proj, w_fr, w_fk, w_fv)):
            w_cast[wi] = I("gpsimd", "ldw2", op(
                "dma_start", out=w_bf[wi], in_=wsrc), 16)
        for qt in reversed(range(NGRP)):
            for pair in range(2):
                hA, hB = 2 * pair, 2 * pair + 1
                nkb = 4 * (qt + 1)
                qsl = slice(qt * 512, (qt + 1) * 512)
                base = ser

                def scores_emit(j):
                    sj = base + j
                    cell = s_cell[sj % 2]
                    ksl = slice(j * 128, (j + 1) * 128)
                    if sj >= 2:
                        W("tensor", "sE", exp_c[sj - 2])
                    E("tensor", op("matmul", cell[:, 0:512],
                                   k_sb[pair][0:64, ksl],
                                   q_sb[pair][0:64, qsl],
                                   start=True, stop=True))
                    sS_c = I("tensor", "sS", op(
                        "matmul", cell[:, 512:1024], k_sb[pair][64:128, ksl],
                        q_sb[pair][64:128, qsl], start=True, stop=True))
                    W("scalar", "sS", sS_c)
                    if sj >= 3:
                        W("scalar", "sV", attv_c[sj - 3])
                    exp_c[sj] = I("scalar", "sE", op(
                        "activation", e_cell[sj % 3][:], s_cell[sj % 2][:],
                        AF.Exp, scale=0.125, bias=s_shiftn[:]))
                    if j >= nkb - 4:
                        o384 = 384 - 128 * (j - (nkb - 4))
                        ecl = e_cell[sj % 3]
                        W("vector", "sE", exp_c[sj])
                        E("vector", op("tensor_mul", ecl[:, 0:512],
                                       ecl[:, 0:512],
                                       s_maskd[:, o384:o384 + 512]))
                        gate_c[sj] = ("sM", I("vector", "sM", op(
                            "tensor_mul", ecl[:, 512:1024], ecl[:, 512:1024],
                            s_maskd[:, o384:o384 + 512])))
                    else:
                        gate_c[sj] = ("sE", exp_c[sj])

                def attv_emit(j, yset):
                    sj = base + j
                    ecl = e_cell[sj % 3]
                    gs, gv = gate_c[sj]
                    if gs == "sM":
                        W("tensor", "sE", exp_c[sj])
                    W("tensor", gs, gv)
                    if j == 0 and ytile >= 2:
                        W("tensor", "dve", y_mult[ytile - 2])
                    E("tensor", op("matmul", y_ps[2 * yset][:],
                                   v_sb[hA][:, j * 65:(j + 1) * 65],
                                   ecl[:, 0:512], start=(j == 0),
                                   stop=(j == nkb - 1)))
                    attv_c[sj] = I("tensor", "sV", op(
                        "matmul", y_ps[2 * yset + 1][:],
                        v_sb[hB][:, j * 65:(j + 1) * 65],
                        ecl[:, 512:1024], start=(j == 0),
                        stop=(j == nkb - 1)))

                yset = ytile % 2
                scores_emit(0)
                if nkb > 1:
                    scores_emit(1)
                for j in range(nkb):
                    if j + 2 < nkb:
                        scores_emit(j + 2)
                    if j == min(4, nkb - 1) and pending_norm[0] is not None:
                        pending_norm[0]()
                        pending_norm[0] = None
                    attv_emit(j, yset)
                ser += nkb

                W("vector", "sV", attv_c[ser - 1])
                if ytile >= 1:
                    W("vector", "dve", y_mult[ytile - 1])
                yA, yB = y_ps[2 * yset], y_ps[2 * yset + 1]
                E("vector", op("reciprocal", rcp[0:1, :], yA[64:65, :]))
                E("vector", op("reciprocal", rcp[64:65, :], yB[64:65, :]))
                E("vector", op("tensor_copy", yev[0:64, :], yA[0:64, :]))
                d_ev = I("vector", "dve", op(
                    "tensor_copy", yev[64:128, :], yB[0:64, :]))

                def mk_norm(qt=qt, pair=pair, n=ytile, yset=yset, d_ev=d_ev):
                    def norm():
                        yA, yB = y_ps[2 * yset], y_ps[2 * yset + 1]
                        W("tensor", "dve", d_ev)
                        E("tensor", op("matmul", yA[0:64, :],
                                       s_ones_f[0:1, 0:64], rcp[0:1, :],
                                       start=True, stop=True))
                        p_bc = I("tensor", "pe", op(
                            "matmul", yB[0:64, :], s_ones64[64:65, :],
                            rcp[64:65, :], start=True, stop=True))
                        W("vector", "pe", p_bc)
                        if n >= 2:
                            W("vector", f"sty{n % 2}", y_dma[n - 2])
                        yt = ynrm[n % 2]
                        E("vector", op("tensor_mul", yt[0:64, :],
                                       yev[0:64, :], yA[0:64, :]))
                        y_mult[n] = I("vector", "dve", op(
                            "tensor_mul", yt[64:128, :], yev[64:128, :],
                            yB[0:64, :]))
                        W("sync", "dve", y_mult[n])
                        iq = qt // QG
                        y_dma[n] = I("sync", f"sty{n % 2}", op(
                            "dma_start",
                            out=cc_in_q[iq][pair * 128:(pair + 1) * 128,
                                            (qt % QG) * 512:
                                            (qt % QG + 1) * 512],
                            in_=yt[:]), 16)
                    return norm
                pending_norm[0] = mk_norm()
                ytile += 1
            if qt % QG == 0:
                if pending_norm[0] is not None:
                    pending_norm[0]()
                    pending_norm[0] = None
                iq = qt // QG
                W("gpsimd", "sty0", cnt["sty0"])
                W("gpsimd", "sty1", cnt["sty1"])
                I("gpsimd", "scc", op(
                    "collective_compute", "AllGather", ALU.bypass,
                    replica_groups=GROUPS, ins=[cc_in_q[iq]],
                    outs=[cc_out_all[iq * C:(iq + 1) * C, :]]))

        ph2_dve = cnt["dve"]

        stw = ExitStack()
        w_projT = sb(stw, "w_projT", [128, 8 * 1024], BF16, side="right")
        w_frT = sb(stw, "w_frT", [128, 8 * 1024], BF16, side="right")
        w_fkT = sb(stw, "w_fkT", [128, 8 * 1024], BF16, side="right")
        w_fvT = sb(stw, "w_fvT", [128, 8 * 1024], BF16, side="right")

        ps2.close()
        st2.close()
        st1.close()

        st4 = ExitStack()
        y_tile = [sb(st4, f"y_tile{i}", [128, 8 * 512], BF16)
                  for i in range(2)]
        y_halo = sb(st4, "y_halo", [128, 8 * 128], BF16)
        xr_t = [sb(st4, f"xr_t{i}", [128, C], F32) for i in range(2)]
        out1 = [sb(st4, f"out1{i}", [128, 4 * C], F32) for i in range(2)]
        h2_tm = [sb(st4, f"h2_tm{i}", [128, C], BF16) for i in range(2)]
        h2_fm = sb(st4, "h2_fm", [128, 8 * 513], BF16)
        hstage = sb(st4, "hstage", [128, 8], BF16)
        p4sq = sb(st4, "p4sq", [128, C], BF16)
        p4st = [sb(st4, f"p4st{i}", [128, 5], F32) for i in range(7)]
        d_mix = sb(st4, "d_mix", [128, 8 * 512], BF16)
        xr_fm = sb(st4, "xr_fm", [128, 8 * 512], BF16)
        xk_fm = sb(st4, "xk_fm", [128, 8 * 512], BF16)
        kact = sb(st4, "kact", [128, 8 * 512], BF16)
        ksc = sb(st4, "ksc", [128, 512], F32)
        r_f32 = sb(st4, "r_f32", [128, C], F32)
        fin = [sb(st4, f"fin{i}", [128, C], F32) for i in range(3)]

        ps4 = ExitStack()
        pr_ps = psum(ps4, "pr_ps", [128, 1024])
        t4_ps = [psum(ps4, f"t4_ps{i}", [128, 128], BF16) for i in range(2)]
        rr_ps = psum(ps4, "rr_ps", [128, 1024])
        kv_ps = psum(ps4, "kv_ps", [128, 1024])

        tr4_evac = {-2: 0, -1: 0}
        t4i_box = [0]
        W("sync", "ldw2", cnt["ldw2"])
        for wi, wdst in enumerate((w_projT, w_frT, w_fkT, w_fvT)):
            for icc in range(8):
                I("sync", "ldwt", op(
                    "dma_start", out=wdst[:, icc * 1024:(icc + 1) * 1024],
                    in_=w_bf[wi][:, icc * 128:(icc + 1) * 128],
                    transpose=True), 16)
        ldwt_all = cnt["ldwt"]

        def sy_pid(e):
            qv = e.partition_id() % 4
            sy_pid.qrow = e.snap(qv * 8, donate=True, min_val=0, max_val=32)
            sy_pid.hrow = e.snap(((qv + 3) % 4) * 8, donate=True,
                                 min_val=0, max_val=32)
        E("sync", sy_pid)
        cc_view = cc_out_all.rearrange("(r b) t -> b r t", b=128)

        resid_d, xres_ld = {}, {}
        p4sq_act = []
        tr4_done = {}
        last_apply_box = [0]

        def ln2_chain(cols, prev_apply):
            W("vector", "dve", prev_apply)
            E("vector", op("tensor_scalar_mul", p4st[2][:, cols],
                           p4st[0][:, cols], 1.0 / C))
            k1 = I("vector", "dve", op("tensor_scalar_mul", p4st[3][:, cols],
                                       p4st[1][:, cols], 1.0 / C))
            W("vector", "dve", k1)
            k2 = I("vector", "dve", op("scalar_tensor_tensor", p4st[4][:, cols],
                                       p4st[2][:, cols], -1.0, p4st[2][:, cols],
                                       ALU.mult, ALU.mult))
            W("vector", "dve", k2)
            k3 = I("vector", "dve", op("tensor_add", p4st[3][:, cols],
                                       p4st[3][:, cols], p4st[4][:, cols]))
            W("scalar", "dve", k3)
            k4 = I("scalar", "act", op("activation", p4st[4][:, cols],
                                       p4st[3][:, cols], AF.Sqrt,
                                       bias=s_eps[:]))
            W("vector", "act", k4)
            k5 = I("vector", "dve", op("reciprocal", p4st[5][:, cols],
                                       p4st[4][:, cols]))
            return k5

        def emit_proj(tile, lhs_fn, stcol, o1, pe_waits, act_waits=()):
            if tile >= 2:
                W("sync", "dve", resid_d[tile - 2])
            xres_ld[tile] = I("sync", f"ldr{tile % 2}", op(
                "dma_start", out=xr_t[tile % 2][:],
                in_=x_res[tile * 128:(tile + 1) * 128, :]), 16)
            for sm, vv in pe_waits:
                W("tensor", sm, vv)
            if tile >= 1:
                W("tensor", "dve", resid_d.get(tile - 1, 0))
            p_pr = None
            for half in range(2):
                o = pr_ps[:, half * 512:(half + 1) * 512]
                for c8 in range(8):
                    E("tensor", op("matmul", o, lhs_fn(c8),
                                   w_projT[:, c8 * 1024 + half * 512:
                                           c8 * 1024 + half * 512 + 512],
                                   start=(c8 == 0), stop=False))
                p_pr = I("tensor", "pe", op(
                    "matmul", o, s_ones_b[0:1, 0:128],
                    s_bproj[:, half * 512:half * 512 + 512],
                    start=False, stop=True))
            W("vector", "pe", p_pr)
            W("vector", f"ldr{tile % 2}", xres_ld[tile])
            for sm, vv in act_waits:
                W("vector", sm, vv)
            E("vector", op("tensor_add", o1[:, 0:512], pr_ps[:, 0:512],
                           xr_t[tile % 2][:, 0:512]))
            resid_d[tile] = I("vector", "dve", op(
                "tensor_add", o1[:, 512:1024], pr_ps[:, 512:1024],
                xr_t[tile % 2][:, 512:1024]))
            W("scalar", "dve", resid_d[tile])
            if p4sq_act:
                W("scalar", "act", p4sq_act[-1])
            a_sq = I("scalar", "act", op(
                "activation", p4sq[:], o1[:], AF.Square,
                accum_out=p4st[1][:, stcol:stcol + 1]))
            p4sq_act.append(a_sq)
            W("vector", "dve", resid_d[tile])
            d_rs = I("vector", "dve", op(
                "reduce_sum", p4st[0][:, stcol:stcol + 1], o1[:], axis=AX.X))
            return a_sq, d_rs

        def emit_apply_tr(tile, stcol, o1, halo):
            W("vector", "pe", tr4_done.get(tile - 2, 0))
            a_ap = I("vector", "dve", op(
                "tensor_scalar", h2_tm[tile % 2][:], o1[:],
                p4st[2][:, stcol:stcol + 1], p4st[5][:, stcol:stcol + 1],
                ALU.subtract, ALU.mult))
            last_apply_box[0] = a_ap
            for c8 in range(8):
                t4i = t4i_box[0]
                if c8 == 0:
                    W("tensor", "dve", a_ap)
                W("tensor", "dve", tr4_evac[t4i - 2])
                p_tr = I("tensor", "pe", op(
                    "transpose", t4_ps[t4i % 2][:],
                    h2_tm[tile % 2][:, c8 * 128:(c8 + 1) * 128], s_ident[:]))
                W("vector", "pe", p_tr)
                if halo:
                    tr4_evac[t4i] = I("vector", "dve", op(
                        "tensor_scalar_mul", h2_fm[:, c8 * 513:c8 * 513 + 1],
                        t4_ps[t4i % 2][:, 127:128], s_halo[:, 0:1]))
                else:
                    sub = (tile - 1) % 4
                    tr4_evac[t4i] = I("vector", "dve", op(
                        "tensor_copy",
                        h2_fm[:, c8 * 513 + 1 + sub * 128:
                              c8 * 513 + 1 + (sub + 1) * 128],
                        t4_ps[t4i % 2][:]))
                t4i_box[0] += 1
            tr4_done[tile] = cnt["pe"]

        W("sync", "scc", 4)
        yhl = I("sync", "ldyh", (
            lambda e: e.dma_start(
                out=y_halo[:].rearrange("p (a t) -> p a t", t=128),
                in_=cc_view[:, bass.ds(sy_pid.hrow, 8),
                            QSPAN - 128:QSPAN])), 8 * 16)
        out1h = out1[1][:, 0:C]
        a_sq, d_rs = emit_proj(
            0, lambda c8: y_halo[:, c8 * 128:(c8 + 1) * 128], 4, out1h,
            pe_waits=(("ldwt", ldwt_all), ("ldyh", yhl)))
        W("vector", "act", a_sq)
        W("vector", "dve", d_rs)
        ch = ln2_chain(slice(4, 5), 0)
        W("vector", "dve", ch)
        emit_apply_tr(0, 4, out1h, halo=True)
        halo_sq_act = a_sq
        halo_apply = last_apply_box[0]

        proj_pe_done = {}
        f2_prev = [0]
        mix_done_of = {}
        stripe_f2 = {}
        stripe_pe_end = 0
        kact_act = {}
        last_sig = {}
        fin_store = {}
        fin_m = {}

        def emit_ffn(sf):
            nonlocal stripe_pe_end, mix_done_of
            o1 = out1[sf % 2]
            for occ in range(8):
                par = occ % 2
                kslc = kv_ps[:, par * 512:(par + 1) * 512]
                if occ == 0:
                    W("tensor", "dve", mix_done_of[sf])
                if occ >= 2:
                    W("tensor", "act", kact_act.get((sf, occ - 2), 0))
                p_k = None
                for icc in range(8):
                    f = op("matmul", kslc,
                           w_fkT[:, icc * 1024 + occ * 128:
                                 icc * 1024 + (occ + 1) * 128],
                           xk_fm[:, icc * 512:(icc + 1) * 512],
                           start=(icc == 0), stop=(icc == 7))
                    if icc == 7:
                        p_k = I("tensor", "pe", f)
                    else:
                        E("tensor", f)
                W("vector", "pe", p_k)
                if occ >= 1:
                    W("vector", "act", kact_act[(sf, occ - 1)])
                elif sf > 0:
                    W("vector", "act", kact_act[(sf - 1, 7)])
                d_rl = I("vector", "dve", op("tensor_scalar_max", ksc[:],
                                             kslc, 0.0))
                W("scalar", "dve", d_rl)
                if sf > 0 and occ == 0:
                    W("scalar", "pe", stripe_pe_end)
                kact_act[(sf, occ)] = I("scalar", "act", op(
                    "activation", kact[:, occ * 512:(occ + 1) * 512], ksc[:],
                    AF.Square))

            for sub in range(4):
                idx = sf * 4 + sub
                if sub == 0:
                    W("tensor", "dve", mix_done_of[sf])
                else:
                    W("tensor", "act", last_sig[idx - 1])
                p_r = None
                for half in range(2):
                    o = rr_ps[:, half * 512:(half + 1) * 512]
                    for c8 in range(8):
                        f = op("matmul", o,
                               xr_fm[:, c8 * 512 + sub * 128:
                                     c8 * 512 + (sub + 1) * 128],
                               w_frT[:, c8 * 1024 + half * 512:
                                     c8 * 1024 + half * 512 + 512],
                               start=(c8 == 0), stop=(c8 == 7))
                        if half == 1 and c8 == 7:
                            p_r = I("tensor", "pe", f)
                        else:
                            E("tensor", f)
                W("scalar", "pe", p_r)
                if idx >= 1:
                    W("scalar", "dve", fin_m[idx - 1])
                last_sig[idx] = I("scalar", "act", op(
                    "activation", r_f32[:], rr_ps[:], AF.Sigmoid))
                if sub == 0:
                    W("tensor", "act", kact_act[(sf, 7)])
                else:
                    W("tensor", "dve", fin_m[idx - 1])
                p_v = None
                for half in range(2):
                    o = kv_ps[:, half * 512:(half + 1) * 512]
                    for c8 in range(8):
                        f = op("matmul", o,
                               kact[:, c8 * 512 + sub * 128:
                                    c8 * 512 + (sub + 1) * 128],
                               w_fvT[:, c8 * 1024 + half * 512:
                                     c8 * 1024 + half * 512 + 512],
                               start=(c8 == 0), stop=(c8 == 7))
                        if half == 1 and c8 == 7:
                            p_v = I("tensor", "pe", f)
                        else:
                            E("tensor", f)
                W("vector", "pe", p_v)
                W("vector", "act", last_sig[idx])
                if f2_prev[0]:
                    W("vector", "dve", f2_prev[0])
                E("vector", op("tensor_mul", fin[0][:, 0:512],
                               kv_ps[:, 0:512], r_f32[:, 0:512]))
                fin_m[idx] = I("vector", "dve", op(
                    "tensor_mul", fin[0][:, 512:1024], kv_ps[:, 512:1024],
                    r_f32[:, 512:1024]))
                W("vector", "dve", fin_m[idx])
                if idx >= 2:
                    W("vector", f"sto{idx % 2}", fin_store[idx - 2])
                fb = fin[1 + idx % 2]
                E("vector", op("tensor_add", fb[:, 0:512], fin[0][:, 0:512],
                               o1[:, sub * 1024:sub * 1024 + 512]))
                f2 = I("vector", "dve", op(
                    "tensor_add", fb[:, 512:1024], fin[0][:, 512:1024],
                    o1[:, sub * 1024 + 512:(sub + 1) * 1024]))
                f2_prev[0] = f2
                W("sync", "dve", f2)
                fin_store[idx] = I("sync", f"sto{idx % 2}", op(
                    "dma_start",
                    out=out_t[idx * 128:(idx + 1) * 128, :], in_=fb[:]), 16)
            stripe_pe_end = cnt["pe"]
            stripe_f2[sf] = f2_prev[0]

        for s in range(NSTR):
            if s >= 2:
                W("sync", "pe", proj_pe_done[s - 2])
            ylast = I("sync", "ldy", (
                lambda e, s=s: e.dma_start(
                    out=y_tile[s % 2][:].rearrange("p (a t) -> p a t", t=512),
                    in_=cc_view[:, bass.ds(sy_pid.qrow, 8),
                                s * 512:(s + 1) * 512])), 8 * 16)

            st_list = []
            for sub in range(4):
                tile = 1 + 4 * s + sub
                pw = [("ldy", ylast)] if sub == 0 else []
                aw = []
                if s >= 2 and sub == 0:
                    aw = [("dve", stripe_f2[s - 2])]
                a_sq, d_rs = emit_proj(
                    tile,
                    (lambda c8, sub=sub, s=s:
                     y_tile[s % 2][:, c8 * 512 + sub * 128:
                                   c8 * 512 + (sub + 1) * 128]),
                    sub, out1[s % 2][:, sub * 1024:(sub + 1) * 1024],
                    pe_waits=pw, act_waits=aw)
                st_list.append((a_sq, d_rs))
            proj_pe_done[s] = cnt["pe"]
            W("vector", "act", st_list[-1][0])
            W("vector", "dve", st_list[-1][1])
            ch = ln2_chain(slice(0, 4), last_apply_box[0])
            W("vector", "dve", ch)

            if s > 0:
                emit_ffn(s - 1)
                W("vector", "dve", mix_done_of[s - 1])
                hv = h2_fm[:].rearrange("p (c w) -> p c w", w=513)
                b0 = I("vector", "dve", op("tensor_copy", hstage[:],
                                           hv[:, :, 512]))
                W("vector", "dve", b0)
                b1 = I("vector", "dve", op("tensor_copy", hv[:, :, 0],
                                           hstage[:]))
                W("vector", "dve", b1)

            for sub in range(4):
                tile = 1 + 4 * s + sub
                emit_apply_tr(tile, sub,
                              out1[s % 2][:, sub * 1024:(sub + 1) * 1024],
                              halo=False)
            evac_done = cnt["dve"]

            W("vector", "dve", evac_done)
            W("vector", "ldmu", ldmu_all)
            if s > 0:
                W("vector", "pe", stripe_pe_end)
            dm = None
            for c8 in range(8):
                cur = h2_fm[:, c8 * 513 + 1: c8 * 513 + 513]
                prv = h2_fm[:, c8 * 513: c8 * 513 + 512]
                dm = I("vector", "dve", op(
                    "tensor_sub", d_mix[:, c8 * 512:(c8 + 1) * 512], cur, prv))
            W("vector", "dve", dm)
            mx = None
            for c8 in range(8):
                prv = h2_fm[:, c8 * 513: c8 * 513 + 512]
                E("vector", op("scalar_tensor_tensor",
                               xr_fm[:, c8 * 512:(c8 + 1) * 512],
                               d_mix[:, c8 * 512:(c8 + 1) * 512],
                               s_mur[:, c8:c8 + 1], prv, ALU.mult, ALU.add))
                mx = I("vector", "dve", op(
                    "scalar_tensor_tensor", xk_fm[:, c8 * 512:(c8 + 1) * 512],
                    d_mix[:, c8 * 512:(c8 + 1) * 512],
                    s_muk[:, c8:c8 + 1], prv, ALU.mult, ALU.add))
            mix_done_of[s] = mx
        emit_ffn(NSTR - 1)

        W("sync", "sto0", cnt["sto0"])
        W("sync", "sto1", cnt["sto1"])
        W("gpsimd", "sto0", cnt["sto0"])
        W("gpsimd", "sto1", cnt["sto1"])

        for eng in ENGS:
            def body(e, ops=tuple(q[eng]), eng=eng):
                for fi, f in enumerate(ops):
                    try:
                        f(e)
                    except Exception as ex:
                        import traceback
                        traceback.print_exc()
                        raise RuntimeError(
                            f"flush fail {eng}[{fi}]: "
                            f"{getattr(f, 'opname', None) or [getattr(d, 'opname', None) for d in (f.__defaults__ or [])]}"
                        ) from ex
            getattr(block, eng)(body)

        ps4.close()
        st4.close()
        stw.close()

    return nc



def _mask_host():
    p = np.arange(128)[:, None]
    g = np.arange(896)[None, :]
    return (p <= g - 384).astype(ml_dtypes.bfloat16)


def prep_inputs(inp, T):
    x = np.ascontiguousarray(np.asarray(inp["L0_attn_x"], dtype=np.float32))
    W_attn = np.asarray(inp["L0_attn_W_attn"], dtype=np.float32)
    b_attn = np.asarray(inp["L0_attn_b_attn"], dtype=np.float32)
    TSL = T // 4
    shared = dict(
        w_proj=np.ascontiguousarray(np.asarray(inp["L0_attn_W_proj"], np.float32)),
        b_proj=np.asarray(inp["L0_attn_b_proj"], np.float32).reshape(1, C),
        w_fr=np.ascontiguousarray(np.asarray(inp["L0_ffn_W_r"], np.float32)),
        w_fk=np.ascontiguousarray(np.asarray(inp["L0_ffn_W_k"], np.float32)),
        w_fv=np.ascontiguousarray(np.asarray(inp["L0_ffn_W_v"], np.float32)),
        mu_r=np.asarray(inp["L0_ffn_mu_r"], np.float32),
        mu_k=np.asarray(inp["L0_ffn_mu_k"], np.float32),
        identb=np.eye(128, dtype=ml_dtypes.bfloat16),
        maskd=_mask_host(),
    )
    in_maps = []
    for c in range(8):
        seq, qq = c // 4, c % 4
        heads = [4 * qq + h for h in range(HPC)]
        rows, brows = [], []
        for sec in range(3):
            for h in heads:
                rows.append(W_attn[sec * C + h * HD: sec * C + (h + 1) * HD])
                brows.append(b_attn[sec * C + h * HD: sec * C + (h + 1) * HD])
        xs = x[seq, :T]
        t0 = qq * TSL
        halo_rows = (np.zeros((128, C), np.float32) if qq == 0
                     else xs[t0 - 128: t0])
        in_maps.append(dict(
            shared,
            x_seq=np.ascontiguousarray(xs),
            x_res=np.ascontiguousarray(
                np.concatenate([halo_rows, xs[t0: t0 + TSL]], axis=0)),
            w_qkv=np.ascontiguousarray(np.concatenate(rows, axis=0)),
            b_qkv=np.concatenate(brows).reshape(1, 768),
            halo_s=np.full((128, 1), 0.0 if qq == 0 else 1.0, np.float32),
        ))
    return in_maps


_NC_CACHE = {}


def kernel(**inputs):
    T = 8192
    if T not in _NC_CACHE:
        _NC_CACHE[T] = build(T)
    nc = _NC_CACHE[T]
    in_maps = prep_inputs(inputs, T)
    res = run_bass_kernel_spmd(nc, in_maps, core_ids=list(range(8)))
    TSL = T // 4
    outp = np.empty((2, T, C), dtype=np.float32)
    for c in range(8):
        seq, qq = c // 4, c % 4
        outp[seq, qq * TSL:(qq + 1) * TSL] = res.results[c]["out"]
    return outp
